# version 13
# baseline (speedup 1.0000x reference)
import numpy as np
import ml_dtypes
import concourse.bass as bass
import concourse.mybir as mybir
from concourse.bass_utils import run_bass_kernel_spmd

F32 = mybir.dt.float32
F16 = mybir.dt.float16
BF16 = mybir.dt.bfloat16
I32 = mybir.dt.int32
AF = mybir.ActivationFunctionType
ALU = mybir.AluOpType

B_, S_, D_ = 4, 4096, 1024
T = 2048
DH = 2048
FH = 3072
TA = 256
NTA = T // TA
TB = 512
NTB = T // TB
TC = 512
NTC = T // TC
PI = float(np.pi)
INV2PI = float(1.0 / (2.0 * np.pi))
M2PI = float(-2.0 * np.pi)
SC127 = float(np.float32(1.0) / np.float32(127.0))
ARENA = 53200

_DTB = {F32: 4, F16: 2, BF16: 2, I32: 4}


class Region:

    def __init__(self, arena, off, end):
        self.arena = arena
        self.off = off
        self.end = end

    def alloc(self, cols, dt, at=None):
        f32c = (cols * _DTB[dt] + 3) // 4
        a = self.off if at is None else at
        if at is None:
            self.off += f32c
            assert self.off <= self.end, f"SBUF overflow: {self.off}>{self.end}"
        v = self.arena[:, a:a + f32c]
        if dt != F32:
            v = v.bitcast(dt)[:, :cols]
        return v


def build_program(debug=False):
    nc = bass.Bass("TRN2", target_bir_lowering=False,
                   detect_race_conditions=False)

    x_d = nc.dram_tensor("x", [D_, T], F32, kind="ExternalInput").ap()
    w_in_d = nc.dram_tensor("w_in", [D_, DH], F32, kind="ExternalInput").ap()
    w_gx_d = nc.dram_tensor("w_gx", [DH, 128], F32, kind="ExternalInput").ap()
    w_ga_d = nc.dram_tensor("w_ga", [DH, 128], F32, kind="ExternalInput").ap()
    w_out_d = nc.dram_tensor("w_out", [DH, D_], F32, kind="ExternalInput").ap()
    wg_d = nc.dram_tensor("wg_t", [D_, FH], I32, kind="ExternalInput").ap()
    wu_d = nc.dram_tensor("wu_t", [D_, FH], I32, kind="ExternalInput").ap()
    wd_d = nc.dram_tensor("wd_t", [FH, D_], I32, kind="ExternalInput").ap()
    b_in_d = nc.dram_tensor("b_in", [128, 16], F32, kind="ExternalInput").ap()
    b_gx_d = nc.dram_tensor("b_gx", [128, 8], F32, kind="ExternalInput").ap()
    b_ga_d = nc.dram_tensor("b_ga", [128, 8], F32, kind="ExternalInput").ap()
    b_out_d = nc.dram_tensor("b_out", [128, 8], F32, kind="ExternalInput").ap()
    aimag_d = nc.dram_tensor("aimag", [128, 8], F32, kind="ExternalInput").ap()
    arp_d = nc.dram_tensor("arp", [128, 8], F32, kind="ExternalInput").ap()
    mask_d = nc.dram_tensor("mask", [128, 1], F32, kind="ExternalInput").ap()
    out_d = nc.dram_tensor("out", [D_, T], F32, kind="ExternalOutput").ap()

    def scr(name, shape, dt):
        if debug:
            return nc.dram_tensor(name, shape, dt, kind="ExternalOutput")
        return nc.dram_tensor(name, shape, dt)

    sp_xtr = scr("sp_xtr", [8, 128, T], F16)
    sp_xti = scr("sp_xti", [8, 128, T], F16)
    sp_cp = scr("sp_cp", [8, 128, T], F16)
    sp_sp = scr("sp_sp", [8, 128, T], F16)
    sp_la = scr("sp_la", [8, 128, T], F16)
    sp_h = scr("sp_h", [16, 128, T], BF16)
    sp_x1 = scr("sp_x1", [NTB, 128, 8 * TB], BF16)
    wgb_d = nc.dram_tensor("wgb_d", [8, 128, FH], BF16)
    wub_d = nc.dram_tensor("wub_d", [8, 128, FH], BF16)
    wdb_d = nc.dram_tensor("wdb_d", [24, 128, D_], BF16)
    cc_in = nc.dram_tensor("cc_in", [128, 16], F32)
    cc_out = nc.dram_tensor("cc_out", [2, 128, 16], F32)

    arena_t = nc.sbuf_tensor("arena", [128, ARENA], F32).__enter__()
    arena = arena_t[:, :]
    ps = [nc.psum_tensor(f"ps{i}", [128, 512], F32).__enter__() for i in range(8)]

    sems = {}
    for n in ("d_vec", "d_w0", "d_w1", "d_x0", "d_x1", "d_spl",
              "d_wo0", "d_wo1", "d_p10", "d_p11", "d_p20", "d_p21",
              "d_cc", "d_h0", "d_h1", "d_b20", "d_b21", "d_xr0", "d_xr1",
              "d_xs0", "d_xs1", "d_cw0", "d_cw1", "d_c10", "d_c11",
              "d_o0", "d_o1", "d_ci0", "d_ci1", "d_co0", "d_co1",
              "d_wd0", "d_wd1", "d_cg0", "d_cg1", "d_dbg",
              "s_t", "s_s", "s_v", "s_g", "cc"):
        sems[n] = nc.semaphore(n).__enter__()
    cnt = {k: 0 for k in sems}
    ops = {e: [] for e in ("sp", "t", "v", "s", "g")}

    def add(eng, fn, waits=(), inc=None, inc_n=1):
        waits = tuple((w, c) for (w, c) in waits
                      if callable(c) or c > 0)
        ops[eng].append((fn, waits, inc, inc_n))
        if inc is not None:
            cnt[inc] += inc_n
            return cnt[inc]
        return None

    P = Region(arena, 0, ARENA)
    vb_in = P.alloc(16, F32)
    vb_gx = P.alloc(8, F32)
    vb_ga = P.alloc(8, F32)
    vb_out = P.alloc(8, F32)
    vaimag = P.alloc(8, F32)
    vm8sp = P.alloc(8, F32)
    vmask = P.alloc(1, F32)
    ones1 = P.alloc(1, F32)
    pio2 = P.alloc(1, F32)
    epsln = P.alloc(1, F32)
    zcol = P.alloc(1, F32)
    th_off = P.alloc(8, F32)
    carry_st = P.alloc(16, F32)
    carry_ga = P.alloc(16, F32)
    carry_eff = P.alloc(16, F32)
    g0rL = P.alloc(8, F32)
    g0iL = P.alloc(8, F32)
    car_a = P.alloc(8, F32)
    car_b = P.alloc(8, F32)
    cpL = P.alloc(8, F16)
    spL = P.alloc(8, F16)
    ones_bf = P.alloc(128, BF16)
    onesA = P.alloc(TA, F32)
    dum_a = P.alloc(64, F32)
    dum_b = P.alloc(64, F32)
    PERSIST_END = P.off

    for dst, src in ((vb_in, b_in_d), (vb_gx, b_gx_d), (vb_ga, b_ga_d),
                     (vb_out, b_out_d), (vaimag, aimag_d), (vm8sp, arp_d),
                     (vmask, mask_d)):
        add("sp", lambda e, d=dst, s=src: e.dma_start(out=d, in_=s),
            inc="d_vec", inc_n=16)
    mk_vec = cnt["d_vec"]

    for tile, val in ((ones1, 1.0), (pio2, PI / 2), (epsln, 1e-5),
                      (zcol, 0.0), (th_off, 0.0), (onesA, 1.0),
                      (dum_a, 0.0), (dum_b, 0.0)):
        add("g", lambda e, t=tile, v=val: e.memset(t, v))
    mk_gmem = add("g", lambda e: e.memset(ones_bf, 1.0), inc="s_g")

    mk_sp0 = add("s", lambda e: e.activation(out=vm8sp, in_=vm8sp,
                                             func=AF.Exp),
                 waits=(("d_vec", mk_vec),), inc="s_s")
    mk_sp0v = add("v", lambda e: e.tensor_scalar(out=vm8sp, in0=vm8sp,
                                                 scalar1=1.0, scalar2=None,
                                                 op0=ALU.add),
                  waits=(("s_s", mk_sp0),), inc="s_v")
    mk_sp1 = add("s", lambda e: e.activation(out=vm8sp, in_=vm8sp,
                                             func=AF.Ln),
                 waits=(("s_v", mk_sp0v),), inc="s_s")
    mk_m8sp = add("v", lambda e: e.tensor_scalar(out=vm8sp, in0=vm8sp,
                                                 scalar1=-8.0, scalar2=None,
                                                 op0=ALU.mult),
                  waits=(("s_s", mk_sp1),), inc="s_v")

    RA = Region(arena, PERSIST_END, ARENA)
    w_in_bf = RA.alloc(8 * DH, BF16)
    wgx_bf = RA.alloc(16 * 128, BF16)
    wga_bf = RA.alloc(16 * 128, BF16)
    x32 = [RA.alloc(8 * TA, F32) for _ in range(2)]
    nsb = RA.alloc(8 * TA, F32)
    xbf = [RA.alloc(8 * TA, BF16) for _ in range(2)]
    xh_sb = [RA.alloc(16 * TA, BF16) for _ in range(2)]
    gate_x = [RA.alloc(8 * TA, BF16) for _ in range(2)]
    gate_a = [RA.alloc(8 * TA, BF16) for _ in range(2)]
    TH_OFFSET = RA.off
    th = RA.alloc(8 * TA, F32)
    kint = RA.alloc(8 * TA, I32, at=TH_OFFSET)
    Th = RA.alloc(8 * TA, F32)
    rho = RA.alloc(8 * TA, F32)
    tsc = RA.alloc(8 * TA, F32)
    thr = RA.alloc(8 * TA, F32)
    thr2b = RA.alloc(8 * TA, F32)
    laS = [RA.alloc(8 * TA, F16) for _ in range(2)]
    cpS = RA.alloc(8 * TA, F16)
    spS = RA.alloc(8 * TA, F16)
    xtrS = RA.alloc(8 * TA, F16)
    xtiS = RA.alloc(8 * TA, F16)
    normb = RA.alloc(8 * TA, BF16)
    g2b = RA.alloc(8 * TA, BF16)
    ub = RA.alloc(8 * TA, BF16)
    vvb = RA.alloc(8 * TA, BF16)
    m1b = RA.alloc(8 * TA, BF16)
    m2b = RA.alloc(8 * TA, BF16)
    m3b = RA.alloc(8 * TA, BF16)
    wst = [Region(arena, TH_OFFSET, ARENA).alloc(DH, F32),
           Region(arena, TH_OFFSET + 2048, ARENA).alloc(DH, F32)]

    mk_wincv = {}
    for k in range(8):
        w = (("s_v", mk_wincv.get(k - 2, 0)),)
        mk = add("sp", lambda e, k=k: e.dma_start(
            out=wst[k % 2], in_=w_in_d[k * 128:(k + 1) * 128, :]),
            waits=w, inc=f"d_w{k % 2}", inc_n=16)
        mk_wincv[k] = add("v", lambda e, k=k: e.tensor_copy(
            out=w_in_bf[:, k * DH:(k + 1) * DH], in_=wst[k % 2]),
            waits=((f"d_w{k % 2}", mk),), inc="s_v")

    mk_gcv = {}
    for wi, (dram, dstb) in enumerate(((w_gx_d, wgx_bf), (w_ga_d, wga_bf))):
        slot = wi % 2
        free_mk = mk_wincv[6 + slot]
        dmk = None
        for q in range(4):
            src = dram[q * 512:(q + 1) * 512, :].rearrange(
                "(j p) m -> p j m", p=128)
            dst = wst[slot][:, q * 512:(q + 1) * 512].rearrange(
                "p (j m) -> p j m", m=128)
            dmk = add("sp", lambda e, s=src, d=dst: e.dma_start(out=d, in_=s),
                      waits=(("s_v", free_mk),), inc=f"d_w{slot}", inc_n=16)
        mk_gcv[wi] = add("v", lambda e, dstb=dstb, slot=slot: e.tensor_copy(
            out=dstb, in_=wst[slot]), waits=((f"d_w{slot}", dmk),), inc="s_v")
    mk_wready = mk_gcv[1]

    mk_xld = {}
    mk_xbf = {-1: 0}
    mk_xhmm = {}
    mk_xhev = {}
    mk_gmm = {}
    mk_gev = {}
    mk_grd = {-2: 0, -1: 0}
    mk_rotV = {-2: 0, -1: 0}
    mk_rotG = {-2: 0, -1: 0}
    mk_spill = {-2: 0, -1: 0}
    mk_laS = {}
    mk_rho = {}
    mk_ns = {-1: 0}
    mk_q = {}
    mk_thr2 = {-1: 0}

    def emit_xload(t_):
        dm = None
        for half in range(2):
            srcx = x_d[half * 512:(half + 1) * 512,
                       t_ * TA:(t_ + 1) * TA].rearrange("(c p) t -> p c t",
                                                        p=128)
            dst = x32[t_ % 2][:, half * 4 * TA:(half + 1) * 4 * TA].rearrange(
                "p (c t) -> p c t", t=TA)
            dm = add("sp", lambda e, s=srcx, d=dst: e.dma_start(out=d, in_=s),
                     waits=((("s_s", lambda t_=t_: mk_xbf[t_ - 2]),)
                            if half == 0 and t_ >= 2 else ()),
                     inc=f"d_x{t_ % 2}", inc_n=16)
        mk_xld[t_] = dm

    def emit_xbf(t_):
        w0 = ((("s_g", mk_gmem), ("d_vec", mk_vec)) if t_ < 2 else ())
        mk_xbf[t_] = add("s", lambda e, t_=t_: e.activation(
            out=xbf[t_ % 2], in_=x32[t_ % 2], func=AF.Identity, bias=zcol),
            waits=w0 + ((f"d_x{t_ % 2}", mk_xld[t_]),
                        ("s_t", lambda t_=t_: mk_xhmm.get((t_ - 2, 15), 0))),
            inc="s_s")

    emit_xload(0)
    emit_xbf(0)
    mk_sin2 = {-1: 0}

    def emit_lateS(t_):
        la = laS[t_ % 2]
        mk_rho[t_] = add("s", lambda e, la=la: e.activation(
            out=rho, in_=la, func=AF.Exp, scale=2.0),
            waits=(("s_v", lambda t_=t_: mk_laS[t_]),
                   ("s_v", lambda t_=t_: mk_ns[t_ - 1]),
                   ("s_v", lambda t_=t_: mk_thr2[t_ - 1])), inc="s_s")
        add("s", lambda e: e.activation(
            out=nsb, in_=rho, func=AF.Ln),
            waits=(("s_v", lambda t_=t_: mk_ns[t_]),), inc="s_s")
        mk_norm_d[t_] = add("s", lambda e: e.activation(
            out=normb, in_=nsb, func=AF.Exp, scale=0.5), inc="s_s")
        mk_s1_d[t_] = add("s", lambda e: e.activation(
            out=spS, in_=thr, func=AF.Sin, scale=float(2 * np.pi)),
            waits=(("s_v", lambda t_=t_: mk_thr1_d[t_]),
                   ("d_spl", lambda t_=t_: mk_spill[t_ - 1])), inc="s_s")
        mk_sin2[t_] = add("s", lambda e: e.activation(
            out=cpS, in_=thr2b, func=AF.Sin, scale=float(2 * np.pi)),
            waits=(("s_v", lambda t_=t_: mk_thr2[t_]),), inc="s_s")

    mk_norm_d = {}
    mk_s1_d = {}
    mk_thr1_d = {}

    for t_ in range(NTA):

        for m in range(16):
            waits = [("s_s", mk_xhev[(t_, m - 2)] if m >= 2
                      else mk_xhev.get((t_ - 1, 14 + m), 0))]
            if m == 0:
                waits.append(("s_s", mk_xbf[t_]))
                if t_ == 0:
                    waits.append(("s_v", mk_wready))
            mk = None
            for k in range(8):
                mk = add("t", lambda e, t_=t_, m=m, k=k: e.matmul(
                    out=ps[m % 2][:, :TA],
                    lhsT=w_in_bf[:, k * DH + m * 128:k * DH + (m + 1) * 128],
                    rhs=xbf[t_ % 2][:, k * TA:(k + 1) * TA],
                    start=(k == 0), stop=(k == 7)),
                    waits=tuple(waits) if k == 0 else (),
                    inc="s_t" if k == 7 else None)
                waits = []
            mk_xhmm[(t_, m)] = mk
            waits = [("s_t", mk_xhmm[(t_, m)])]
            if m == 0:
                waits += [("s_t", mk_gmm.get((t_ - 2, 15), 0)),
                          ("s_v", mk_rotV[t_ - 2]),
                          ("s_g", mk_rotG[t_ - 2])]
                if t_ == 0:
                    waits.append(("d_vec", mk_vec))
            mk_xhev[(t_, m)] = add("s", lambda e, t_=t_, m=m: e.activation(
                out=xh_sb[t_ % 2][:, m * TA:(m + 1) * TA],
                in_=ps[m % 2][:, :TA], func=AF.Identity,
                bias=vb_in[:, m:m + 1]),
                waits=tuple(waits), inc="s_s")

        for gi in range(2):
            for n in range(8):
                idx = gi * 8 + n
                waits = [("s_s", mk_xhev[(t_, 2 * n + 1)]),
                         ("s_s", mk_gev[(t_, idx - 2)] if idx >= 2
                          else mk_gev.get((t_ - 1, 14 + idx), 0))]
                wb = wgx_bf if gi == 0 else wga_bf
                mk = None
                for j in range(2):
                    kc = 2 * n + j
                    mk = add("t", lambda e, t_=t_, wb=wb, kc=kc, idx=idx:
                             e.matmul(
                                 out=ps[2 + idx % 2][:, :TA],
                                 lhsT=wb[:, kc * 128:(kc + 1) * 128],
                                 rhs=xh_sb[t_ % 2][:, kc * TA:(kc + 1) * TA],
                                 start=(kc % 2 == 0), stop=(kc % 2 == 1)),
                             waits=tuple(waits) if j == 0 else (),
                             inc="s_t" if j == 1 else None)
                    waits = []
                mk_gmm[(t_, idx)] = mk
                waits = [("s_t", mk_gmm[(t_, idx)])]
                if idx == 0:
                    waits.append(("s_v", mk_grd[t_ - 2]))
                gdst = gate_x[t_ % 2] if gi == 0 else gate_a[t_ % 2]
                gb = vb_gx if gi == 0 else vb_ga
                mk_gev[(t_, idx)] = add("s", lambda e, gdst=gdst, gb=gb, n=n,
                                        idx=idx: e.activation(
                    out=gdst[:, n * TA:(n + 1) * TA],
                    in_=ps[2 + idx % 2][:, :TA], func=AF.Sigmoid,
                    bias=gb[:, n:n + 1]),
                    waits=tuple(waits), inc="s_s")

        gates_done = mk_gev[(t_, 15)]
        if t_ > 0:
            emit_lateS(t_ - 1)

        ga = gate_a[t_ % 2]
        gx = gate_x[t_ % 2]
        la = laS[t_ % 2]
        mk = None
        for c in range(8):
            waits = []
            if c == 0:
                waits = [("s_s", gates_done), ("d_spl", mk_spill[t_ - 1])]
                if t_ == 0:
                    waits.append(("s_v", mk_m8sp))
            mk = add("v", lambda e, c=c, la=la, ga=ga: e.tensor_scalar(
                out=la[:, c * TA:(c + 1) * TA],
                in0=ga[:, c * TA:(c + 1) * TA],
                scalar1=vm8sp[:, c:c + 1], scalar2=None, op0=ALU.mult),
                waits=tuple(waits), inc="s_v")
        mk_laS[t_] = mk
        for c in range(8):
            add("v", lambda e, c=c, ga=ga: e.tensor_scalar(
                out=th[:, c * TA:(c + 1) * TA],
                in0=ga[:, c * TA:(c + 1) * TA],
                scalar1=vaimag[:, c:c + 1], scalar2=None, op0=ALU.mult),
                waits=((("d_vec", mk_vec),) if t_ == 0 and c == 0 else ()),
                inc="s_v")
        if t_ + 1 < NTA:
            emit_xload(t_ + 1)
            emit_xbf(t_ + 1)
        add("v", lambda e: e.tensor_scalar(
            out=nsb, in0=rho, scalar1=-1.0, scalar2=1.0,
            op0=ALU.mult, op1=ALU.add),
            waits=(("s_s", lambda t_=t_: mk_rho[t_]),), inc="s_v")
        mk_cl = add("v", lambda e: e.tensor_scalar(
            out=rho, in0=nsb, scalar1=1e-6, scalar2=None, op0=ALU.max),
            inc="s_v")
        mk_ns[t_] = mk_cl
        mk_g2 = add("v", lambda e, gx=gx: e.tensor_tensor(
            out=g2b, in0=gx, in1=normb, op=ALU.mult),
            waits=(("s_s", lambda t_=t_: mk_norm_d[t_]),), inc="s_v")
        mk_grd[t_] = mk_g2
        for c in range(8):
            waits = []
            if t_ == 0 and c == 0:
                waits = [("s_g", mk_gmem)]
            add("v", lambda e, c=c: e.tensor_tensor_scan(
                out=Th[:, c * TA:(c + 1) * TA], data0=onesA,
                data1=th[:, c * TA:(c + 1) * TA],
                initial=th_off[:, c:c + 1], op0=ALU.mult, op1=ALU.add),
                waits=tuple(waits), inc="s_v")
        add("v", lambda e: e.tensor_scalar(
            out=tsc, in0=Th, scalar1=INV2PI, scalar2=None, op0=ALU.mult),
            inc="s_v")
        add("v", lambda e: e.tensor_copy(out=kint, in_=tsc), inc="s_v")
        for c in range(8):
            add("v", lambda e, c=c: e.tensor_copy(
                out=th_off[:, c:c + 1],
                in_=Th[:, c * TA + TA - 1:c * TA + TA]), inc="s_v")
        add("v", lambda e: e.scalar_tensor_tensor(
            out=thr, in0=kint, scalar=-1.0, in1=tsc,
            op0=ALU.mult, op1=ALU.add),
            waits=(("s_s", lambda t_=t_: mk_sin2[t_ - 1]),), inc="s_v")
        add("v", lambda e: e.tensor_scalar(
            out=rho, in0=thr, scalar1=0.5, scalar2=None, op0=ALU.is_gt),
            inc="s_v")
        mk_thr1_d[t_] = add("v", lambda e: e.scalar_tensor_tensor(
            out=thr, in0=rho, scalar=-1.0, in1=thr,
            op0=ALU.mult, op1=ALU.add), inc="s_v")
        add("v", lambda e: e.tensor_scalar(
            out=tsc, in0=Th, scalar1=INV2PI, scalar2=0.25,
            op0=ALU.mult, op1=ALU.add), inc="s_v")
        add("v", lambda e: e.tensor_copy(out=kint, in_=tsc), inc="s_v")
        add("v", lambda e: e.scalar_tensor_tensor(
            out=thr2b, in0=kint, scalar=-1.0, in1=tsc,
            op0=ALU.mult, op1=ALU.add),
            waits=(("s_s", lambda t_=t_: mk_s1_d[t_]),), inc="s_v")
        add("v", lambda e: e.tensor_scalar(
            out=rho, in0=thr2b, scalar1=0.5, scalar2=None, op0=ALU.is_gt),
            inc="s_v")
        mk_thr2[t_] = add("v", lambda e: e.scalar_tensor_tensor(
            out=thr2b, in0=rho, scalar=-1.0, in1=thr2b,
            op0=ALU.mult, op1=ALU.add), inc="s_v")
        add("v", lambda e: e.tensor_tensor(
            out=ub, in0=g2b, in1=cpS, op=ALU.mult),
            waits=(("s_s", lambda t_=t_: mk_sin2[t_]),), inc="s_v")
        mk_uv = add("v", lambda e: e.tensor_tensor(
            out=vvb, in0=g2b, in1=spS, op=ALU.mult), inc="s_v")
        add("v", lambda e, t_=t_: e.tensor_tensor(
            out=m1b, in0=ub, in1=xh_sb[t_ % 2][:, 0:8 * TA], op=ALU.mult),
            inc="s_v")
        add("v", lambda e, t_=t_: e.tensor_tensor(
            out=m2b, in0=vvb, in1=xh_sb[t_ % 2][:, 8 * TA:16 * TA],
            op=ALU.mult), inc="s_v")
        mk_m3 = add("g", lambda e, t_=t_: e.tensor_tensor(
            out=m3b, in0=ub, in1=xh_sb[t_ % 2][:, 8 * TA:16 * TA],
            op=ALU.mult), waits=(("s_v", cnt["s_v"]),), inc="s_g")
        mk_m4 = add("v", lambda e, t_=t_: e.tensor_tensor(
            out=xtiS, in0=vvb, in1=xh_sb[t_ % 2][:, 0:8 * TA], op=ALU.mult),
            waits=(("d_spl", lambda t_=t_: mk_spill[t_ - 1]),), inc="s_v")
        mk_xtr = add("v", lambda e: e.tensor_tensor(
            out=xtrS, in0=m1b, in1=m2b, op=ALU.add), inc="s_v")
        mk_rotG[t_] = mk_m3
        mk_xti = add("v", lambda e: e.scalar_tensor_tensor(
            out=xtiS, in0=xtiS, scalar=-1.0, in1=m3b,
            op0=ALU.mult, op1=ALU.add),
            waits=(("s_g", mk_m3),), inc="s_v")
        mk_rotV[t_] = mk_xti

        mk = None
        for i, (stile, dram) in enumerate(((xtrS, sp_xtr), (xtiS, sp_xti),
                                           (cpS, sp_cp), (spS, sp_sp),
                                           (la, sp_la))):
            sv = stile.rearrange("p (c a) -> p c a", a=TA)
            dv = dram.ap()[:, :, t_ * TA:(t_ + 1) * TA].rearrange(
                "c p a -> p c a")
            for hf in range(2):
                mk = add("sp", lambda e, s=sv[:, hf * 4:(hf + 1) * 4, :],
                         d=dv[:, hf * 4:(hf + 1) * 4, :]: e.dma_start(
                    out=d, in_=s),
                    waits=(("s_v", mk_xti),) if (i == 0 and hf == 0) else (),
                    inc="d_spl", inc_n=16)
        mk_spill[t_] = mk

    emit_lateS(NTA - 1)
    A_done_v = cnt["s_v"]
    A_done_s = cnt["s_s"]
    A_done_t = cnt["s_t"]
    A_done_g = cnt["s_g"]
    A_done_dma = [(n, cnt[n]) for n in ("d_vec", "d_w0", "d_w1", "d_x0",
                                        "d_x1", "d_spl")]

    RB = Region(arena, PERSIST_END, ARENA)
    w_out_bf = RB.alloc(16 * D_, BF16)
    B2_START = RB.off
    wst2 = [RB.alloc(D_, F32) for _ in range(2)]
    bxtr = [RB.alloc(T, F16) for _ in range(3)]
    bxti = [RB.alloc(T, F16) for _ in range(3)]
    bcp = [RB.alloc(T, F16) for _ in range(3)]
    bsp = [RB.alloc(T, F16) for _ in range(3)]
    bla = [RB.alloc(T, F16) for _ in range(3)]
    brho = [RB.alloc(T, F32) for _ in range(2)]
    bgr = [RB.alloc(T, F16) for _ in range(2)]
    bgi = [RB.alloc(T, F16) for _ in range(2)]
    g0sc = bgr
    g0sd = bgi
    bh1 = [RB.alloc(T, BF16) for _ in range(2)]
    bh2 = [RB.alloc(T, BF16) for _ in range(2)]
    bh3 = [RB.alloc(T, BF16) for _ in range(2)]
    bh4 = [RB.alloc(T, BF16) for _ in range(2)]
    bhr = [RB.alloc(T, BF16) for _ in range(2)]
    bhi = [RB.alloc(T, BF16) for _ in range(2)]
    wcvI = wcvB = None

    barrier_B = tuple([("s_v", A_done_v), ("s_s", A_done_s),
                       ("s_t", A_done_t), ("s_g", A_done_g)] + A_done_dma)

    mk_wocv = {}
    wo_state = {"i": 0}

    def emit_wo(n):
        for _ in range(n):
            kc = wo_state["i"]
            if kc >= 16:
                return
            wo_state["i"] += 1
            waits = list(barrier_B) if kc == 0 else []
            if kc >= 2:
                waits.append(("s_s", mk_wocv[kc - 2]))
            mk = add("sp", lambda e, kc=kc: e.dma_start(
                out=wst2[kc % 2], in_=w_out_d[kc * 128:(kc + 1) * 128, :]),
                waits=tuple(waits), inc=f"d_wo{kc % 2}", inc_n=16)
            mk_wocv[kc] = add("s", lambda e, kc=kc: e.activation(
                out=w_out_bf[:, kc * D_:(kc + 1) * D_], in_=wst2[kc % 2],
                func=AF.Identity, bias=zcol),
                waits=((f"d_wo{kc % 2}", mk),), inc="s_s")

    wcv_jobs = []
    for dram, dstd, n1, wid in ((wd_d, wdb_d, 24, D_), (wg_d, wgb_d, 8, FH),
                                (wu_d, wub_d, 8, FH)):
        for k in range(n1):
            for j in range(0, wid, 1024):
                wcv_jobs.append((dram, dstd, k, j))
    mk_wcv_cv = {}
    mk_wcv_out = {-2: 0, -1: 0}
    wcv_state = {"i": 0}
    wcv_marks = {}

    def emit_wcv(n):
        for _ in range(n):
            widx = wcv_state["i"]
            if widx >= len(wcv_jobs):
                return
            wcv_state["i"] += 1
            dram, dstd, k, j = wcv_jobs[widx]
            sl = widx % 2
            if widx < 24:
                lq = "sp"
                ci = f"d_ci{sl}"
                waits = [("s_g", mk_wcv_cv[widx - 2])] if widx >= 2 else []
            else:
                lq = "g"
                ci = f"d_cg{sl}"
                waits = []
            if widx == 0:
                waits += list(barrier_B)
            mki = add(lq, lambda e, dram=dram, k=k, j=j, sl=sl:
                      e.dma_start(out=wcvI[sl],
                                  in_=dram[k * 128:(k + 1) * 128,
                                           j:j + 1024]),
                      waits=tuple(waits), inc=ci, inc_n=16)
            mk_wcv_cv[widx] = add("g", lambda e, sl=sl: e.tensor_copy(
                out=wcvB[sl], in_=wcvI[sl]),
                waits=((ci, mki),
                       (f"d_co{sl}", mk_wcv_out[widx - 2])), inc="s_g")
            mk_wcv_out[widx] = add("g", lambda e, dstd=dstd, k=k, j=j,
                                   sl=sl: e.dma_start(
                out=dstd.ap()[k][:, j:j + 1024], in_=wcvB[sl]),
                inc=f"d_co{sl}", inc_n=16)
            if widx == 23:
                wcv_marks["wd"] = (cnt["d_co0"], cnt["d_co1"])
            elif widx == 47:
                wcv_marks["wg"] = (cnt["d_co0"], cnt["d_co1"])
            elif widx == 71:
                wcv_marks["wu"] = (cnt["d_co0"], cnt["d_co1"])

    def gather(plane, c, dstbuf, queue, sem, extra_waits=()):
        return add(queue, lambda e, s=plane.ap()[c], d=dstbuf: e.dma_start(
            out=d, in_=s), waits=tuple(extra_waits), inc=sem, inc_n=16)

    mk_p1ld = {}
    mk_p1s = {}
    mk_p1 = {-3: 0, -2: 0, -1: 0}
    for c in range(8):
        w = [("s_v", mk_p1[c - 3])]
        if c == 0:
            w += list(barrier_B)
        sem = f"d_p1{c % 2}"
        gather(sp_xtr, c, bxtr[c % 3], "s", sem, extra_waits=w)
        gather(sp_xti, c, bxti[c % 3], "s", sem)
        mk_p1ld[c] = gather(sp_la, c, bla[c % 3], "s", sem)
        mk_p1s[c] = add("s", lambda e, c=c: e.activation(
            out=brho[c % 2], in_=bla[c % 3], func=AF.Exp),
            waits=((f"d_p1{c % 2}", mk_p1ld[c]), ("s_v", mk_p1[c - 2])),
            inc="s_s")
        add("v", lambda e, c=c: e.tensor_tensor_scan(
            out=g0sc[c % 2], data0=brho[c % 2], data1=bxtr[c % 3],
            initial=zcol[:, 0:1], op0=ALU.mult, op1=ALU.add),
            waits=(("s_s", mk_p1s[c]),), inc="s_v")
        add("v", lambda e, c=c: e.tensor_tensor_scan(
            out=g0sd[c % 2], data0=brho[c % 2], data1=bxti[c % 3],
            initial=zcol[:, 0:1], op0=ALU.mult, op1=ALU.add), inc="s_v")
        if c > 0:
            add("v", lambda e, c=c: e.tensor_copy(
                out=g0rL[:, c - 1:c], in_=g0sc[(c - 1) % 2][:, T - 1:T]),
                inc="s_v")
            mk_p1[c - 1] = add("v", lambda e, c=c: e.tensor_copy(
                out=g0iL[:, c - 1:c], in_=g0sd[(c - 1) % 2][:, T - 1:T]),
                inc="s_v")
        emit_wcv(3)
        emit_wo(2)
    emit_wo(16)
    mk_wo_done = mk_wocv[15]
    for _ in range(4):
        add("v", lambda e: e.tensor_copy(out=dum_b, in_=dum_a))
    add("v", lambda e: e.tensor_copy(
        out=g0rL[:, 7:8], in_=g0sc[1][:, T - 1:T]), inc="s_v")
    mk_p1[7] = add("v", lambda e: e.tensor_copy(
        out=g0iL[:, 7:8], in_=g0sd[1][:, T - 1:T]), inc="s_v")
    for _ in range(4):
        add("v", lambda e: e.tensor_copy(out=dum_b, in_=dum_a))

    cl_src = sp_cp.ap()[:, :, T - 1:T].rearrange("c p o -> p c o")
    sl_src = sp_sp.ap()[:, :, T - 1:T].rearrange("c p o -> p c o")
    mk_cpl = add("g", lambda e: e.dma_start(
        out=cpL.rearrange("p (c o) -> p c o", o=1), in_=cl_src,
        allow_slow_non_contiguous=True),
        waits=(("d_spl", mk_spill[NTA - 1]),), inc="d_cc", inc_n=16)
    mk_spl = add("g", lambda e: e.dma_start(
        out=spL.rearrange("p (c o) -> p c o", o=1), in_=sl_src,
        allow_slow_non_contiguous=True),
        inc="d_cc", inc_n=16)

    add("v", lambda e: e.tensor_tensor(out=car_a, in0=g0rL, in1=cpL,
                                       op=ALU.mult),
        waits=(("s_v", mk_p1[7]), ("d_cc", mk_spl)), inc="s_v")
    add("v", lambda e: e.tensor_tensor(out=car_b, in0=g0iL, in1=spL,
                                       op=ALU.mult), inc="s_v")
    for _ in range(4):
        add("v", lambda e: e.tensor_copy(out=dum_b, in_=dum_a))
    add("v", lambda e: e.tensor_tensor(out=carry_st[:, 0:8], in0=car_a,
                                       in1=car_b, op=ALU.subtract),
        inc="s_v")
    add("v", lambda e: e.tensor_tensor(out=car_a, in0=g0iL, in1=cpL,
                                       op=ALU.mult), inc="s_v")
    add("v", lambda e: e.tensor_tensor(out=car_b, in0=g0rL, in1=spL,
                                       op=ALU.mult), inc="s_v")
    for _ in range(4):
        add("v", lambda e: e.tensor_copy(out=dum_b, in_=dum_a))
    add("v", lambda e: e.tensor_tensor(
        out=carry_st[:, 8:16], in0=car_a, in1=car_b, op=ALU.add), inc="s_v")
    for _ in range(4):
        add("v", lambda e: e.tensor_copy(out=dum_b, in_=dum_a))
    mk_carry = add("v", lambda e: e.tensor_copy(out=dum_b, in_=dum_a),
                   inc="s_v")

    mk_ccw = add("g", lambda e: e.dma_start(out=cc_in.ap(), in_=carry_st),
                 waits=(("s_v", mk_carry),), inc="d_cc", inc_n=16)
    add("g", lambda e: e.collective_compute(
        "AllGather", ALU.bypass,
        replica_groups=[[0, 1], [2, 3], [4, 5], [6, 7]],
        ins=[cc_in.ap().opt()], outs=[cc_out.ap().opt()]),
        waits=(("d_cc", mk_ccw),), inc="cc")
    mk_ccr = add("g", lambda e: e.dma_start(out=carry_ga, in_=cc_out.ap()[0]),
                 waits=(("cc", 1),), inc="d_cc", inc_n=16)
    add("v", lambda e: e.tensor_scalar(
        out=carry_eff, in0=carry_ga, scalar1=vmask[:, 0:1], scalar2=None,
        op0=ALU.mult),
        waits=(("d_cc", mk_ccr), ("d_vec", mk_vec)), inc="s_v")
    for _ in range(10):
        add("v", lambda e: e.tensor_copy(out=dum_b, in_=dum_a))

    mk_p2ld = {}
    mk_p2s = {}
    mk_p2scan = {-2: 0, -1: 0}
    mk_p2V = {-2: 0, -1: 0}
    mk_p2G = {-2: 0, -1: 0}
    mk_u2 = {}
    mk_hsp = {-2: 0, -1: 0}

    def emit_comb(c):
        mk_hi = None
        mk_hr = add("v", lambda e, c=c: e.tensor_tensor(
            out=bhr[c % 2], in0=bh1[c % 2], in1=bh2[c % 2],
            op=ALU.subtract),
            waits=(("s_g", mk_u2[c]), (f"d_h{c % 2}", mk_hsp[c - 2])),
            inc="s_v")
        mk_hi = add("v", lambda e, c=c: e.tensor_tensor(
            out=bhi[c % 2], in0=bh3[c % 2], in1=bh4[c % 2], op=ALU.add),
            waits=(("s_g", mk_p2G[c]),), inc="s_v")
        mk_p2V[c] = mk_hi
        add("s", lambda e, c=c: e.dma_start(out=sp_h.ap()[c],
                                            in_=bhr[c % 2]),
            waits=(("s_v", mk_hi),), inc=f"d_h{c % 2}", inc_n=16)
        mk_hsp[c] = add("s", lambda e, c=c: e.dma_start(
            out=sp_h.ap()[8 + c], in_=bhi[c % 2]), inc=f"d_h{c % 2}",
            inc_n=16)

    mk_p2V[-3] = mk_p2G[-3] = 0
    for c in range(8):
        w = [("s_v", mk_p2V[c - 3]), ("s_g", mk_p2G[c - 3])]
        if c == 0:
            w.append(("s_v", mk_p1[7]))
        sem = f"d_p2{c % 2}"
        gather(sp_xtr, c, bxtr[c % 3], "sp", sem, extra_waits=w)
        gather(sp_xti, c, bxti[c % 3], "sp", sem)
        gather(sp_cp, c, bcp[c % 3], "sp", sem)
        gather(sp_sp, c, bsp[c % 3], "sp", sem)
        mk_p2ld[c] = gather(sp_la, c, bla[c % 3], "sp", sem)
        mk_p2s[c] = add("s", lambda e, c=c: e.activation(
            out=brho[c % 2], in_=bla[c % 3], func=AF.Exp),
            waits=((f"d_p2{c % 2}", mk_p2ld[c]), ("s_v", mk_p2scan[c - 2]),
                   ("s_s", mk_wo_done if c == 0 else 0)), inc="s_s")
        add("v", lambda e, c=c: e.tensor_tensor_scan(
            out=bgr[c % 2], data0=brho[c % 2], data1=bxtr[c % 3],
            initial=carry_eff[:, c:c + 1], op0=ALU.mult, op1=ALU.add),
            waits=(("s_s", mk_p2s[c]), ("s_g", mk_p2G[c - 2])), inc="s_v")
        mk_p2scan[c] = add("v", lambda e, c=c: e.tensor_tensor_scan(
            out=bgi[c % 2], data0=brho[c % 2], data1=bxti[c % 3],
            initial=carry_eff[:, 8 + c:9 + c], op0=ALU.mult, op1=ALU.add),
            inc="s_v")
        add("v", lambda e, c=c: e.tensor_tensor(
            out=bh1[c % 2], in0=bgr[c % 2], in1=bcp[c % 3], op=ALU.mult),
            inc="s_v")
        mk_u2[c] = add("g", lambda e, c=c: e.tensor_tensor(
            out=bh2[c % 2], in0=bgi[c % 2], in1=bsp[c % 3], op=ALU.mult),
            waits=(("s_v", mk_p2scan[c]),), inc="s_g")
        add("g", lambda e, c=c: e.tensor_tensor(
            out=bh3[c % 2], in0=bgi[c % 2], in1=bcp[c % 3], op=ALU.mult),
            inc="s_g")
        mk_p2G[c] = add("g", lambda e, c=c: e.tensor_tensor(
            out=bh4[c % 2], in0=bgr[c % 2], in1=bsp[c % 3], op=ALU.mult),
            inc="s_g")
        if c > 0:
            emit_comb(c - 1)
    emit_comb(7)

    B1_done_v = cnt["s_v"]
    B1_done_s = cnt["s_s"]
    B1_done_g = cnt["s_g"]
    B1_done_dma = [(n, cnt[n]) for n in ("d_wo0", "d_wo1", "d_p10", "d_p11",
                                         "d_p20", "d_p21", "d_cc", "d_h0",
                                         "d_h1")]

    R2 = Region(arena, B2_START, ARENA)
    bhin = [R2.alloc(16 * TB, BF16) for _ in range(2)]
    bx2 = [R2.alloc(8 * TB, F32) for _ in range(2)]
    br1 = R2.alloc(8 * TB, F32)
    br1b = R2.alloc(8 * TB, BF16)
    bsq = R2.alloc(8 * TB, BF16)
    bmean = R2.alloc(TB, F32)
    bmm = R2.alloc(TB, F32)
    bvar = R2.alloc(TB, F32)
    bstd = R2.alloc(TB, F32)
    brstd = R2.alloc(TB, F32)
    btd = R2.alloc(8 * TB, F32)
    bx1 = [R2.alloc(8 * TB, BF16) for _ in range(2)]
    RW = Region(arena, max(R2.off, RB.off), ARENA)
    wcvI = [RW.alloc(1024, I32) for _ in range(2)]
    wcvB = [RW.alloc(1024, BF16) for _ in range(2)]

    barrier_B2 = tuple([("s_v", B1_done_v), ("s_s", B1_done_s),
                        ("s_g", B1_done_g)] + B1_done_dma)
    mk_hld = {}
    mk_xrl = {}
    mk_ymm = {}
    mk_yev = {}
    mk_x1 = {}
    mk_x1sp = {-2: 0, -1: 0}
    mk_lnev = {-1: 0}
    first_b2 = True
    for t_ in range(NTB):
        waits = list(barrier_B2) if first_b2 else []
        if t_ >= 2:
            waits.append(("s_t", mk_ymm[(t_ - 2, 7)]))
        mk = None
        for kc in range(16):
            mk = add("sp", lambda e, t_=t_, kc=kc: e.dma_start(
                out=bhin[t_ % 2][:, kc * TB:(kc + 1) * TB],
                in_=sp_h.ap()[kc][:, t_ * TB:(t_ + 1) * TB]),
                waits=tuple(waits) if kc == 0 else (),
                inc=f"d_b2{t_ % 2}", inc_n=16)
            waits = []
            first_b2 = False
        mk_hld[t_] = mk
        mk = None
        for half in range(2):
            src = x_d[half * 512:(half + 1) * 512,
                      t_ * TB:(t_ + 1) * TB].rearrange("(c p) t -> p c t",
                                                       p=128)
            dst = bx2[t_ % 2][:, half * 4 * TB:(half + 1) * 4 * TB].rearrange(
                "p (c t) -> p c t", t=TB)
            w2 = (("s_v", mk_yev.get((t_ - 2, 7), 0)),) if half == 0 else ()
            mk = add("sp", lambda e, s=src, d=dst: e.dma_start(out=d, in_=s),
                     waits=w2, inc=f"d_xr{t_ % 2}", inc_n=16)
        mk_xrl[t_] = mk

        for m in range(8):
            waits = [("s_v", mk_yev[(t_, m - 2)] if m >= 2
                      else mk_yev.get((t_ - 1, 6 + m), 0))]
            if m == 0:
                waits.append((f"d_b2{t_ % 2}", mk_hld[t_]))
                if t_ == 0:
                    waits.append(("s_s", mk_wo_done))
            mk = None
            for kc in range(16):
                mk = add("t", lambda e, t_=t_, m=m, kc=kc: e.matmul(
                    out=ps[m % 2][:, :TB],
                    lhsT=w_out_bf[:, kc * D_ + m * 128:
                                  kc * D_ + (m + 1) * 128],
                    rhs=bhin[t_ % 2][:, kc * TB:(kc + 1) * TB],
                    start=(kc == 0), stop=(kc == 15)),
                    waits=tuple(waits) if kc == 0 else (),
                    inc="s_t" if kc == 15 else None)
                waits = []
            mk_ymm[(t_, m)] = mk
            waits = [("s_t", mk_ymm[(t_, m)])]
            if m == 0:
                waits.append((f"d_xr{t_ % 2}", mk_xrl[t_]))
            mk_yev[(t_, m)] = add("v", lambda e, t_=t_, m=m:
                                  e.scalar_tensor_tensor(
                out=br1[:, m * TB:(m + 1) * TB], in0=ps[m % 2][:, :TB],
                scalar=vb_out[:, m:m + 1],
                in1=bx2[t_ % 2][:, m * TB:(m + 1) * TB],
                op0=ALU.add, op1=ALU.add),
                waits=tuple(waits), inc="s_v")
        add("v", lambda e: e.tensor_copy(out=br1b, in_=br1), inc="s_v")
        mk_sq = add("v", lambda e: e.tensor_tensor(
            out=bsq, in0=br1b, in1=br1b, op=ALU.mult), inc="s_v")
        waits = [("s_v", mk_sq), ("s_s", mk_lnev[t_ - 1])]
        for kc in range(8):
            add("t", lambda e, kc=kc: e.matmul(
                out=ps[2][:, :TB], lhsT=ones_bf,
                rhs=br1b[:, kc * TB:(kc + 1) * TB],
                start=(kc == 0), stop=(kc == 7)),
                waits=tuple(waits) if kc == 0 else ())
            waits = []
        mk = None
        for kc in range(8):
            mk = add("t", lambda e, kc=kc: e.matmul(
                out=ps[3][:, :TB], lhsT=ones_bf,
                rhs=bsq[:, kc * TB:(kc + 1) * TB],
                start=(kc == 0), stop=(kc == 7)),
                inc="s_t" if kc == 7 else None)
        mk_lnmm = mk
        mk_mean = add("s", lambda e: e.activation(
            out=bmean, in_=ps[2][:, :TB], func=AF.Identity,
            scale=float(1.0 / 1024.0), bias=zcol),
            waits=(("s_t", mk_lnmm),), inc="s_s")
        add("v", lambda e: e.tensor_tensor(
            out=bmm, in0=bmean, in1=bmean, op=ALU.mult),
            waits=(("s_s", mk_mean),), inc="s_v")
        mk_var = add("v", lambda e: e.scalar_tensor_tensor(
            out=bvar, in0=ps[3][:, :TB], scalar=float(1.0 / 1024.0),
            in1=bmm, op0=ALU.mult, op1=ALU.subtract), inc="s_v")
        mk_lnev[t_] = add("s", lambda e: e.activation(
            out=bstd, in_=bvar, func=AF.Sqrt, bias=epsln),
            waits=(("s_v", mk_var),), inc="s_s")
        add("v", lambda e: e.reciprocal(out=brstd, in_=bstd),
            waits=(("s_s", mk_lnev[t_]),), inc="s_v")
        for m in range(8):
            add("v", lambda e, m=m: e.scalar_tensor_tensor(
                out=btd[:, m * TB:(m + 1) * TB], in0=bmean, scalar=-1.0,
                in1=br1[:, m * TB:(m + 1) * TB], op0=ALU.mult, op1=ALU.add),
                inc="s_v")
        mk = None
        for m in range(8):
            w3 = ((f"d_xs{t_ % 2}", mk_x1sp[t_ - 2]),) if m == 0 else ()
            mk = add("v", lambda e, t_=t_, m=m: e.tensor_tensor(
                out=bx1[t_ % 2][:, m * TB:(m + 1) * TB],
                in0=btd[:, m * TB:(m + 1) * TB], in1=brstd, op=ALU.mult),
                waits=w3, inc="s_v")
        mk_x1[t_] = mk
        mk_x1sp[t_] = add("s", lambda e, t_=t_: e.dma_start(
            out=sp_x1.ap()[t_], in_=bx1[t_ % 2]),
            waits=(("s_v", mk_x1[t_]),), inc=f"d_xs{t_ % 2}", inc_n=16)
        emit_wcv(12)

    emit_wcv(len(wcv_jobs))
    B2_done_v = cnt["s_v"]
    B2_done_s = cnt["s_s"]
    B2_done_t = cnt["s_t"]
    B2_done_dma = [(n, cnt[n]) for n in ("d_b20", "d_b21", "d_xr0", "d_xr1",
                                         "d_xs0", "d_xs1")]

    RC = Region(arena, PERSIST_END, ARENA)
    wgb = RC.alloc(8 * FH, BF16)
    wub = RC.alloc(8 * FH, BF16)
    wdr = [RC.alloc(4 * D_, BF16) for _ in range(2)]
    bx1in = [RC.alloc(8 * TC, BF16) for _ in range(2)]
    BHF_OFF = RC.off
    bhf = RC.alloc(24 * TC, BF16)
    bsil = [RC.alloc(TC, F32) for _ in range(2)]
    bh1f = RC.alloc(TC, F32)
    br2 = RC.alloc(8 * TC, F32)
    br2b = RC.alloc(8 * TC, BF16)
    bsq2 = RC.alloc(8 * TC, BF16)
    cmean = RC.alloc(TC, F32)
    cmm = RC.alloc(TC, F32)
    cvar = RC.alloc(TC, F32)
    crstd = RC.alloc(TC, F32)
    cstd = cmm
    btd2 = RC.alloc(8 * TC, F32, at=BHF_OFF)

    barrier_C = tuple([("s_v", B2_done_v), ("s_s", B2_done_s),
                       ("s_t", B2_done_t)] + B2_done_dma)
    mk_wgld = mk_wuld = None
    for k in range(8):
        w = (list(barrier_C) + [("d_co0", wcv_marks["wg"][0]),
                                ("d_co1", wcv_marks["wg"][1])]
             if k == 0 else [])
        mk_wgld = add("s", lambda e, k=k: e.dma_start(
            out=wgb[:, k * FH:(k + 1) * FH], in_=wgb_d.ap()[k]),
            waits=tuple(w), inc="d_cw0", inc_n=16)
    for k in range(8):
        w = ([("d_co0", wcv_marks["wu"][0]), ("d_co1", wcv_marks["wu"][1])]
             if k == 0 else [])
        mk_wuld = add("s", lambda e, k=k: e.dma_start(
            out=wub[:, k * FH:(k + 1) * FH], in_=wub_d.ap()[k]),
            waits=tuple(w), inc="d_cw1", inc_n=16)

    mk_x1ld = {}
    mk_gu = {}
    mk_sil = {}
    mk_h = {}
    mk_dev = {}
    mk_dng = {}
    mk_wdld = {}
    mk_out = {-2: 0, -1: 0}
    mk_mean2 = {-1: 0}
    mk_var2 = {-1: 0}
    mk_lnev2 = {-1: 0}
    first_csp = True
    for t_ in range(NTC):
        w2 = [(f"d_xs{t_ % 2}", cnt[f"d_xs{t_ % 2}"]),
              ("s_v", mk_dev.get((t_ - 2, 7), 0))]
        if first_csp:
            w2 += list(barrier_C)
            first_csp = False
        mk_x1ld[t_] = add("sp", lambda e, t_=t_: e.dma_start(
            out=bx1in[t_ % 2], in_=sp_x1.ap()[t_]),
            waits=tuple(w2), inc=f"d_c1{t_ % 2}", inc_n=16)

        for g in range(6):
            wdsrc = wdb_d.ap()[g * 4:(g + 1) * 4].rearrange("g p d -> p g d")
            w3 = [("s_t", lambda t_=t_, g=g: mk_dng.get(
                (t_, g - 2), mk_dng.get((t_ - 1, g + 4), 0)))]
            if t_ == 0 and g == 0:
                w3 += [("d_co0", wcv_marks["wd"][0]),
                       ("d_co1", wcv_marks["wd"][1])]
            mk_wdld[(t_, g)] = add("sp", lambda e, s=wdsrc, g=g: e.dma_start(
                out=wdr[g % 2].rearrange("p (g d) -> p g d", d=D_), in_=s),
                waits=tuple(w3), inc=f"d_wd{g % 2}", inc_n=16)

        for hc in range(24):
            waits = [("s_s", mk_sil[(t_, hc - 2)] if hc >= 2
                      else mk_sil.get((t_ - 1, 22 + hc), 0)),
                     ("s_v", mk_h[(t_, hc - 2)] if hc >= 2
                      else mk_h.get((t_ - 1, 22 + hc), 0))]
            if hc == 0:
                waits.append((f"d_c1{t_ % 2}", mk_x1ld[t_]))
                waits.append(("s_v", mk_dev.get((t_ - 1, 3), 0)))
                waits.append(("s_s", mk_mean2[t_ - 1]))
                waits.append(("s_v", mk_var2[t_ - 1]))
                if t_ == 0:
                    waits.append(("d_cw0", mk_wgld))
            for kc in range(8):
                add("t", lambda e, t_=t_, hc=hc, kc=kc: e.matmul(
                    out=ps[hc % 2][:, :TC],
                    lhsT=wgb[:, kc * FH + hc * 128:kc * FH + (hc + 1) * 128],
                    rhs=bx1in[t_ % 2][:, kc * TC:(kc + 1) * TC],
                    start=(kc == 0), stop=(kc == 7)),
                    waits=tuple(waits) if kc == 0 else ())
                waits = []
            if t_ == 0 and hc == 0:
                waits = [("d_cw1", mk_wuld)]
            mk = None
            for kc in range(8):
                mk = add("t", lambda e, t_=t_, hc=hc, kc=kc: e.matmul(
                    out=ps[2 + hc % 2][:, :TC],
                    lhsT=wub[:, kc * FH + hc * 128:kc * FH + (hc + 1) * 128],
                    rhs=bx1in[t_ % 2][:, kc * TC:(kc + 1) * TC],
                    start=(kc == 0), stop=(kc == 7)),
                    waits=tuple(waits) if kc == 0 else (),
                    inc="s_t" if kc == 7 else None)
                waits = []
            mk_gu[(t_, hc)] = mk
            mk_sil[(t_, hc)] = add("s", lambda e, hc=hc: e.activation(
                out=bsil[hc % 2], in_=ps[hc % 2][:, :TC], func=AF.Sigmoid,
                scale=SC127),
                waits=(("s_t", mk_gu[(t_, hc)]),
                       ("s_v", mk_h[(t_, hc - 2)] if hc >= 2
                        else mk_h.get((t_ - 1, 22 + hc), 0))), inc="s_s")
            add("v", lambda e, hc=hc: e.scalar_tensor_tensor(
                out=bh1f, in0=ps[hc % 2][:, :TC], scalar=SC127,
                in1=bsil[hc % 2], op0=ALU.mult, op1=ALU.mult),
                waits=(("s_s", mk_sil[(t_, hc)]),), inc="s_v")
            mk_h[(t_, hc)] = add("v", lambda e, hc=hc: e.scalar_tensor_tensor(
                out=bhf[:, hc * TC:(hc + 1) * TC], in0=ps[2 + hc % 2][:, :TC],
                scalar=SC127, in1=bh1f, op0=ALU.mult, op1=ALU.mult),
                inc="s_v")

        mk_acc = {}
        for g in range(6):
            gw = [(f"d_wd{g % 2}", mk_wdld[(t_, g)])]
            if g == 0:
                gw.append(("s_v", mk_h[(t_, 23)]))
                for m in range(8):
                    gw.append(("s_v", mk_dev.get((t_ - 1, m), 0)))
                gw.append(("s_s", mk_mean2[t_ - 1]))
                gw.append(("s_v", mk_var2[t_ - 1]))
            mk = None
            for m in range(8):
                waits = list(gw) if m == 0 else []
                if g > 0:
                    waits.append(mk_acc[(g - 1, m)])
                for kc in range(4):
                    mk = add("t", lambda e, t_=t_, g=g, m=m, kc=kc: e.matmul(
                        out=ps[m][:, :TC],
                        lhsT=wdr[g % 2][:, kc * D_ + m * 128:
                                        kc * D_ + (m + 1) * 128],
                        rhs=bhf[:, (g * 4 + kc) * TC:(g * 4 + kc + 1) * TC],
                        start=(kc == 0), stop=(kc == 3)),
                        waits=tuple(waits) if kc == 0 else (),
                        inc="s_t" if kc == 3 else None)
                    waits = []
                if g == 0:
                    c = add("s", lambda e, m=m: e.activation(
                        out=br2[:, m * TC:(m + 1) * TC], in_=ps[m][:, :TC],
                        func=AF.Identity, bias=zcol),
                        waits=(("s_t", mk),), inc="s_s")
                    mk_acc[(g, m)] = ("s_s", c)
                else:
                    c = add("v", lambda e, m=m: e.scalar_tensor_tensor(
                        out=br2[:, m * TC:(m + 1) * TC], in0=ps[m][:, :TC],
                        scalar=1.0, in1=br2[:, m * TC:(m + 1) * TC],
                        op0=ALU.mult, op1=ALU.add),
                        waits=(("s_t", mk), mk_acc[(g - 1, m)]), inc="s_v")
                    mk_acc[(g, m)] = ("s_v", c)
            mk_dng[(t_, g)] = mk
        for m in range(8):
            w4 = [mk_acc[(5, m)]]
            if m == 0:
                w4.append((f"d_o{(t_ - 1) % 2}", mk_out[t_ - 1]))
            mk_dev[(t_, m)] = add("v", lambda e, t_=t_, m=m:
                                  e.scalar_tensor_tensor(
                out=br2[:, m * TC:(m + 1) * TC], in0=br2[:, m * TC:(m + 1) * TC],
                scalar=SC127, in1=bx1in[t_ % 2][:, m * TC:(m + 1) * TC],
                op0=ALU.mult, op1=ALU.add),
                waits=tuple(w4), inc="s_v")

        if debug:
            add("g", lambda e, t_=t_: e.dma_start(
                out=nc.dram_tensor(f"dbg_r2_{t_}", [128, 8 * TC], F32,
                                   kind="ExternalOutput").ap(),
                in_=br2), waits=(("s_v", mk_dev[(t_, 7)]),),
                inc="d_dbg", inc_n=16)
        add("s", lambda e: e.activation(out=br2b, in_=br2,
                                        func=AF.Identity, bias=zcol),
            waits=(("s_v", mk_dev[(t_, 7)]),), inc="s_s")
        mk_sq2 = add("s", lambda e: e.activation(
            out=bsq2, in_=br2, func=AF.Square), inc="s_s")
        waits = [("s_s", mk_sq2), ("s_s", mk_lnev2[t_ - 1])]
        for kc in range(8):
            add("t", lambda e, kc=kc: e.matmul(
                out=ps[0][:, :TC], lhsT=ones_bf,
                rhs=br2b[:, kc * TC:(kc + 1) * TC],
                start=(kc == 0), stop=(kc == 7)),
                waits=tuple(waits) if kc == 0 else ())
            waits = []
        mk = None
        for kc in range(8):
            mk = add("t", lambda e, kc=kc: e.matmul(
                out=ps[1][:, :TC], lhsT=ones_bf,
                rhs=bsq2[:, kc * TC:(kc + 1) * TC],
                start=(kc == 0), stop=(kc == 7)),
                inc="s_t" if kc == 7 else None)
        mk_lnmm2 = mk
        mk_mean2[t_] = add("s", lambda e: e.activation(
            out=cmean, in_=ps[0][:, :TC], func=AF.Identity,
            scale=float(1.0 / 1024.0), bias=zcol),
            waits=(("s_t", mk_lnmm2),), inc="s_s")
        add("v", lambda e: e.tensor_tensor(
            out=cmm, in0=cmean, in1=cmean, op=ALU.mult),
            waits=(("s_s", mk_mean2[t_]),), inc="s_v")
        mk_var2[t_] = add("v", lambda e: e.scalar_tensor_tensor(
            out=cvar, in0=ps[1][:, :TC], scalar=float(1.0 / 1024.0),
            in1=cmm, op0=ALU.mult, op1=ALU.subtract), inc="s_v")
        mk_lnev2[t_] = add("s", lambda e: e.activation(
            out=cstd, in_=cvar, func=AF.Sqrt, bias=epsln),
            waits=(("s_v", mk_var2[t_]),), inc="s_s")
        add("v", lambda e: e.reciprocal(out=crstd, in_=cstd),
            waits=(("s_s", mk_lnev2[t_]),), inc="s_v")
        for m in range(8):
            add("v", lambda e, m=m: e.scalar_tensor_tensor(
                out=btd2[:, m * TC:(m + 1) * TC], in0=cmean, scalar=-1.0,
                in1=br2[:, m * TC:(m + 1) * TC], op0=ALU.mult, op1=ALU.add),
                inc="s_v")
        mk = None
        for m in range(8):
            w5 = (("d_dbg", cnt["d_dbg"]),) if (debug and m == 0) else ()
            mk = add("v", lambda e, m=m: e.tensor_tensor(
                out=br2[:, m * TC:(m + 1) * TC],
                in0=btd2[:, m * TC:(m + 1) * TC], in1=crstd, op=ALU.mult),
                waits=w5, inc="s_v")
        mk_oev = mk
        mk = None
        for half in range(2):
            srcb = br2[:, half * 4 * TC:(half + 1) * 4 * TC].rearrange(
                "p (c t) -> p c t", t=TC)
            dstb = out_d[half * 512:(half + 1) * 512,
                         t_ * TC:(t_ + 1) * TC].rearrange("(c p) t -> p c t",
                                                          p=128)
            mk = add("g", lambda e, s=srcb, d=dstb: e.dma_start(out=d, in_=s),
                     waits=(("s_v", mk_oev),) if half == 0 else (),
                     inc=f"d_o{t_ % 2}", inc_n=16)
        mk_out[t_] = mk

    if debug:
        dbg_g = nc.dram_tensor("dbg_g0", [128, 4 * T + 16], F32,
                               kind="ExternalOutput").ap()
        add("g", lambda e: e.dma_start(out=dbg_g[:, 0:T], in_=g0sc[1]),
            waits=(("d_o0", cnt["d_o0"]), ("d_o1", cnt["d_o1"])),
            inc="d_dbg", inc_n=16)
        add("g", lambda e: e.dma_start(out=dbg_g[:, T:2 * T], in_=g0sd[1]),
            inc="d_dbg", inc_n=16)
        add("g", lambda e: e.dma_start(out=dbg_g[:, 2 * T:2 * T + 8],
                                       in_=g0rL), inc="d_dbg", inc_n=16)
        add("g", lambda e: e.dma_start(out=dbg_g[:, 2 * T + 8:2 * T + 16],
                                       in_=g0iL), inc="d_dbg", inc_n=16)
        dbg_d = nc.dram_tensor("dbg_carry", [128, 48], F32,
                               kind="ExternalOutput").ap()
        add("g", lambda e: e.dma_start(out=dbg_d[:, 0:16], in_=carry_st),
            waits=(("d_o0", cnt["d_o0"]), ("d_o1", cnt["d_o1"])),
            inc="d_dbg", inc_n=16)
        add("g", lambda e: e.dma_start(out=dbg_d[:, 16:32], in_=carry_eff),
            inc="d_dbg", inc_n=16)
        add("g", lambda e: e.dma_start(
            out=dbg_d[:, 32:36], in_=cpL.bitcast(F32)[:, 0:4]),
            inc="d_dbg", inc_n=16)
        add("g", lambda e: e.dma_start(
            out=dbg_d[:, 36:40], in_=spL.bitcast(F32)[:, 0:4]),
            inc="d_dbg", inc_n=16)

    add("g", lambda e: e.memset(dum_b, 0.0),
        waits=(("d_o0", cnt["d_o0"]), ("d_o1", cnt["d_o1"]),
               ("d_dbg", cnt["d_dbg"])))

    with nc.Block() as block:
        def emit(eng_key):
            def go(engine):
                for fn, waits, inc, inc_n in ops[eng_key]:
                    for wname, wcnt in waits:
                        if callable(wcnt):
                            wcnt = wcnt()
                        if wcnt > 0:
                            engine.wait_ge(sems[wname], wcnt)
                    instr = fn(engine)
                    if inc is not None:
                        if instr is None:
                            engine.sem_inc(sems[inc], inc_n)
                        else:
                            instr.then_inc(sems[inc], inc_n)
            return go

        block.sync(emit("sp"))
        block.tensor(emit("t"))
        block.vector(emit("v"))
        block.scalar(emit("s"))
        block.gpsimd(emit("g"))

    return nc


_prog_cache = {}


def _get_prog(debug=False):
    if debug not in _prog_cache:
        _prog_cache[debug] = build_program(debug)
    return _prog_cache[debug]


def _vec128(a, n):
    return np.ascontiguousarray(np.asarray(a, np.float32).reshape(n, 128).T)


def kernel(L0_attn_x, L0_attn_w_in, L0_attn_b_in, L0_attn_w_gx, L0_attn_b_gx,
           L0_attn_w_ga, L0_attn_b_ga, L0_attn_a_real_param,
           L0_attn_a_imag_param, L0_attn_w_out, L0_attn_b_out,
           L0_ffn_w_gate, L0_ffn_w_up, L0_ffn_w_down, _debug=False):
    x = np.asarray(L0_attn_x, np.float32)
    shared = {
        "w_in": np.ascontiguousarray(np.asarray(L0_attn_w_in, np.float32)),
        "w_gx": np.ascontiguousarray(
            np.asarray(L0_attn_w_gx, np.float32).reshape(DH, 128)),
        "w_ga": np.ascontiguousarray(
            np.asarray(L0_attn_w_ga, np.float32).reshape(DH, 128)),
        "w_out": np.ascontiguousarray(np.asarray(L0_attn_w_out, np.float32)),
        "wg_t": np.ascontiguousarray(np.asarray(L0_ffn_w_gate, np.int32).T),
        "wu_t": np.ascontiguousarray(np.asarray(L0_ffn_w_up, np.int32).T),
        "wd_t": np.ascontiguousarray(np.asarray(L0_ffn_w_down, np.int32).T),
        "b_in": _vec128(L0_attn_b_in, 16),
        "b_gx": _vec128(L0_attn_b_gx, 8),
        "b_ga": _vec128(L0_attn_b_ga, 8),
        "b_out": _vec128(L0_attn_b_out, 8),
        "aimag": _vec128(L0_attn_a_imag_param, 8),
        "arp": _vec128(L0_attn_a_real_param, 8),
    }
    in_maps = []
    for k in range(8):
        b, half = k // 2, k % 2
        m = dict(shared)
        m["x"] = np.ascontiguousarray(x[b, half * T:(half + 1) * T, :].T)
        m["mask"] = np.full((128, 1), float(half), np.float32)
        in_maps.append(m)

    nc = _get_prog(_debug)
    res = run_bass_kernel_spmd(nc, in_maps, core_ids=list(range(8)))

    y = np.empty((B_, S_, D_), np.float32)
    for k in range(8):
        b, half = k // 2, k % 2
        y[b, half * T:(half + 1) * T, :] = res.results[k]["out"].T
    if _debug:
        return y, res
    return y


# version 14 (final)
# speedup vs baseline: 1.0857x; 1.0857x over previous
import numpy as np
import ml_dtypes
import concourse.bass as bass
import concourse.mybir as mybir
from concourse.bass_utils import run_bass_kernel_spmd

F32 = mybir.dt.float32
F16 = mybir.dt.float16
BF16 = mybir.dt.bfloat16
I32 = mybir.dt.int32
AF = mybir.ActivationFunctionType
ALU = mybir.AluOpType

B_, S_, D_ = 4, 4096, 1024
T = 2048
DH = 2048
FH = 3072
TA = 256
NTA = T // TA
TB = 512
NTB = T // TB
TC = 512
NTC = T // TC
PI = float(np.pi)
INV2PI = float(1.0 / (2.0 * np.pi))
M2PI = float(-2.0 * np.pi)
SC127 = float(np.float32(1.0) / np.float32(127.0))
ARENA = 53200

_DTB = {F32: 4, F16: 2, BF16: 2, I32: 4}


class Region:

    def __init__(self, arena, off, end):
        self.arena = arena
        self.off = off
        self.end = end

    def alloc(self, cols, dt, at=None):
        f32c = (cols * _DTB[dt] + 3) // 4
        a = self.off if at is None else at
        if at is None:
            self.off += f32c
            assert self.off <= self.end, f"SBUF overflow: {self.off}>{self.end}"
        v = self.arena[:, a:a + f32c]
        if dt != F32:
            v = v.bitcast(dt)[:, :cols]
        return v


def build_program(debug=False):
    nc = bass.Bass("TRN2", target_bir_lowering=False,
                   detect_race_conditions=False)

    x_d = nc.dram_tensor("x", [D_, T], F32, kind="ExternalInput").ap()
    w_in_d = nc.dram_tensor("w_in", [D_, DH], F32, kind="ExternalInput").ap()
    w_gx_d = nc.dram_tensor("w_gx", [DH, 128], F32, kind="ExternalInput").ap()
    w_ga_d = nc.dram_tensor("w_ga", [DH, 128], F32, kind="ExternalInput").ap()
    w_out_d = nc.dram_tensor("w_out", [DH, D_], F32, kind="ExternalInput").ap()
    wg_d = nc.dram_tensor("wg_t", [D_, FH], I32, kind="ExternalInput").ap()
    wu_d = nc.dram_tensor("wu_t", [D_, FH], I32, kind="ExternalInput").ap()
    wd_d = nc.dram_tensor("wd_t", [FH, D_], I32, kind="ExternalInput").ap()
    b_in_d = nc.dram_tensor("b_in", [128, 16], F32, kind="ExternalInput").ap()
    b_gx_d = nc.dram_tensor("b_gx", [128, 8], F32, kind="ExternalInput").ap()
    b_ga_d = nc.dram_tensor("b_ga", [128, 8], F32, kind="ExternalInput").ap()
    b_out_d = nc.dram_tensor("b_out", [128, 8], F32, kind="ExternalInput").ap()
    aimag_d = nc.dram_tensor("aimag", [128, 8], F32, kind="ExternalInput").ap()
    arp_d = nc.dram_tensor("arp", [128, 8], F32, kind="ExternalInput").ap()
    mask_d = nc.dram_tensor("mask", [128, 1], F32, kind="ExternalInput").ap()
    out_d = nc.dram_tensor("out", [D_, T], F32, kind="ExternalOutput").ap()

    def scr(name, shape, dt):
        if debug:
            return nc.dram_tensor(name, shape, dt, kind="ExternalOutput")
        return nc.dram_tensor(name, shape, dt)

    sp_xtr = scr("sp_xtr", [8, 128, T], F16)
    sp_xti = scr("sp_xti", [8, 128, T], F16)
    sp_cp = scr("sp_cp", [8, 128, T], F16)
    sp_sp = scr("sp_sp", [8, 128, T], F16)
    sp_la = scr("sp_la", [8, 128, T], F16)
    sp_h = scr("sp_h", [16, 128, T], BF16)
    sp_x1 = scr("sp_x1", [NTB, 128, 8 * TB], BF16)
    wgb_d = nc.dram_tensor("wgb_d", [8, 128, FH], BF16)
    wub_d = nc.dram_tensor("wub_d", [8, 128, FH], BF16)
    wdb_d = nc.dram_tensor("wdb_d", [24, 128, D_], BF16)
    cc_in = nc.dram_tensor("cc_in", [128, 16], F32)
    cc_out = nc.dram_tensor("cc_out", [2, 128, 16], F32)

    arena_t = nc.sbuf_tensor("arena", [128, ARENA], F32).__enter__()
    arena = arena_t[:, :]
    ps = [nc.psum_tensor(f"ps{i}", [128, 512], F32).__enter__() for i in range(8)]

    sems = {}
    for n in ("d_vec", "d_w0", "d_w1", "d_x0", "d_x1", "d_spl",
              "d_wo0", "d_wo1", "d_p10", "d_p11", "d_p20", "d_p21",
              "d_cc", "d_h0", "d_h1", "d_b20", "d_b21", "d_xr0", "d_xr1",
              "d_xs0", "d_xs1", "d_cw0", "d_cw1", "d_c10", "d_c11",
              "d_o0", "d_o1", "d_ci0", "d_ci1", "d_co0", "d_co1",
              "d_wd0", "d_wd1", "d_dbg",
              "s_t", "s_s", "s_v", "s_g", "cc"):
        sems[n] = nc.semaphore(n).__enter__()
    cnt = {k: 0 for k in sems}
    ops = {e: [] for e in ("sp", "t", "v", "s", "g")}

    def add(eng, fn, waits=(), inc=None, inc_n=1):
        waits = tuple((w, c) for (w, c) in waits
                      if callable(c) or c > 0)
        ops[eng].append((fn, waits, inc, inc_n))
        if inc is not None:
            cnt[inc] += inc_n
            return cnt[inc]
        return None

    P = Region(arena, 0, ARENA)
    vb_in = P.alloc(16, F32)
    vb_gx = P.alloc(8, F32)
    vb_ga = P.alloc(8, F32)
    vb_out = P.alloc(8, F32)
    vaimag = P.alloc(8, F32)
    vm8sp = P.alloc(8, F32)
    vmask = P.alloc(1, F32)
    ones1 = P.alloc(1, F32)
    pio2 = P.alloc(1, F32)
    epsln = P.alloc(1, F32)
    zcol = P.alloc(1, F32)
    th_off = P.alloc(8, F32)
    carry_st = P.alloc(16, F32)
    carry_ga = P.alloc(16, F32)
    carry_eff = P.alloc(16, F32)
    g0rL = P.alloc(8, F32)
    g0iL = P.alloc(8, F32)
    car_a = P.alloc(8, F32)
    car_b = P.alloc(8, F32)
    cpL = P.alloc(8, F16)
    spL = P.alloc(8, F16)
    ones_bf = P.alloc(128, BF16)
    onesA = P.alloc(TA, F32)
    dum_a = P.alloc(64, F32)
    dum_b = P.alloc(64, F32)
    PERSIST_END = P.off

    for dst, src in ((vb_in, b_in_d), (vb_gx, b_gx_d), (vb_ga, b_ga_d),
                     (vb_out, b_out_d), (vaimag, aimag_d), (vm8sp, arp_d),
                     (vmask, mask_d)):
        add("sp", lambda e, d=dst, s=src: e.dma_start(out=d, in_=s),
            inc="d_vec", inc_n=16)
    mk_vec = cnt["d_vec"]

    for tile, val in ((ones1, 1.0), (pio2, PI / 2), (epsln, 1e-5),
                      (zcol, 0.0), (th_off, 0.0), (onesA, 1.0),
                      (dum_a, 0.0), (dum_b, 0.0)):
        add("g", lambda e, t=tile, v=val: e.memset(t, v))
    mk_gmem = add("g", lambda e: e.memset(ones_bf, 1.0), inc="s_g")

    mk_sp0 = add("s", lambda e: e.activation(out=vm8sp, in_=vm8sp,
                                             func=AF.Exp),
                 waits=(("d_vec", mk_vec),), inc="s_s")
    mk_sp0v = add("v", lambda e: e.tensor_scalar(out=vm8sp, in0=vm8sp,
                                                 scalar1=1.0, scalar2=None,
                                                 op0=ALU.add),
                  waits=(("s_s", mk_sp0),), inc="s_v")
    mk_sp1 = add("s", lambda e: e.activation(out=vm8sp, in_=vm8sp,
                                             func=AF.Ln),
                 waits=(("s_v", mk_sp0v),), inc="s_s")
    mk_m8sp = add("v", lambda e: e.tensor_scalar(out=vm8sp, in0=vm8sp,
                                                 scalar1=-8.0, scalar2=None,
                                                 op0=ALU.mult),
                  waits=(("s_s", mk_sp1),), inc="s_v")

    RA = Region(arena, PERSIST_END, ARENA)
    w_in_bf = RA.alloc(8 * DH, BF16)
    wgx_bf = RA.alloc(16 * 128, BF16)
    wga_bf = RA.alloc(16 * 128, BF16)
    x32 = [RA.alloc(8 * TA, F32) for _ in range(2)]
    nsb = RA.alloc(8 * TA, F32)
    xbf = [RA.alloc(8 * TA, BF16) for _ in range(2)]
    xh_sb = [RA.alloc(16 * TA, BF16) for _ in range(2)]
    gate_x = [RA.alloc(8 * TA, BF16) for _ in range(2)]
    gate_a = [RA.alloc(8 * TA, BF16) for _ in range(2)]
    TH_OFFSET = RA.off
    th = RA.alloc(8 * TA, F32)
    kint = RA.alloc(8 * TA, I32, at=TH_OFFSET)
    Th = RA.alloc(8 * TA, F32)
    rho = RA.alloc(8 * TA, F32)
    tsc = RA.alloc(8 * TA, F32)
    thr = RA.alloc(8 * TA, F32)
    thr2b = RA.alloc(8 * TA, F32)
    laS = [RA.alloc(8 * TA, F16) for _ in range(2)]
    cpS = RA.alloc(8 * TA, F16)
    spS = RA.alloc(8 * TA, F16)
    xtrS = RA.alloc(8 * TA, F16)
    xtiS = RA.alloc(8 * TA, F16)
    normb = RA.alloc(8 * TA, BF16)
    g2b = RA.alloc(8 * TA, BF16)
    ub = RA.alloc(8 * TA, BF16)
    vvb = RA.alloc(8 * TA, BF16)
    m1b = RA.alloc(8 * TA, BF16)
    m2b = RA.alloc(8 * TA, BF16)
    m3b = RA.alloc(8 * TA, BF16)
    wst = [Region(arena, TH_OFFSET, ARENA).alloc(DH, F32),
           Region(arena, TH_OFFSET + 2048, ARENA).alloc(DH, F32)]

    mk_wincv = {}
    for k in range(8):
        w = (("s_v", mk_wincv.get(k - 2, 0)),)
        mk = add("sp", lambda e, k=k: e.dma_start(
            out=wst[k % 2], in_=w_in_d[k * 128:(k + 1) * 128, :]),
            waits=w, inc=f"d_w{k % 2}", inc_n=16)
        mk_wincv[k] = add("v", lambda e, k=k: e.tensor_copy(
            out=w_in_bf[:, k * DH:(k + 1) * DH], in_=wst[k % 2]),
            waits=((f"d_w{k % 2}", mk),), inc="s_v")

    mk_gcv = {}
    for wi, (dram, dstb) in enumerate(((w_gx_d, wgx_bf), (w_ga_d, wga_bf))):
        slot = wi % 2
        free_mk = mk_wincv[6 + slot]
        dmk = None
        for q in range(4):
            src = dram[q * 512:(q + 1) * 512, :].rearrange(
                "(j p) m -> p j m", p=128)
            dst = wst[slot][:, q * 512:(q + 1) * 512].rearrange(
                "p (j m) -> p j m", m=128)
            dmk = add("sp", lambda e, s=src, d=dst: e.dma_start(out=d, in_=s),
                      waits=(("s_v", free_mk),), inc=f"d_w{slot}", inc_n=16)
        mk_gcv[wi] = add("v", lambda e, dstb=dstb, slot=slot: e.tensor_copy(
            out=dstb, in_=wst[slot]), waits=((f"d_w{slot}", dmk),), inc="s_v")
    mk_wready = mk_gcv[1]

    mk_xld = {}
    mk_xbf = {-1: 0}
    mk_xhmm = {}
    mk_xhev = {}
    mk_gmm = {}
    mk_gev = {}
    mk_grd = {-2: 0, -1: 0}
    mk_rotV = {-2: 0, -1: 0}
    mk_rotG = {-2: 0, -1: 0}
    mk_spill = {-2: 0, -1: 0}
    mk_laS = {}
    mk_rho = {}
    mk_ns = {-1: 0}
    mk_q = {}
    mk_thr2 = {-1: 0}

    def emit_xload(t_):
        dm = None
        for half in range(2):
            srcx = x_d[half * 512:(half + 1) * 512,
                       t_ * TA:(t_ + 1) * TA].rearrange("(c p) t -> p c t",
                                                        p=128)
            dst = x32[t_ % 2][:, half * 4 * TA:(half + 1) * 4 * TA].rearrange(
                "p (c t) -> p c t", t=TA)
            dm = add("sp", lambda e, s=srcx, d=dst: e.dma_start(out=d, in_=s),
                     waits=((("s_s", lambda t_=t_: mk_xbf[t_ - 2]),)
                            if half == 0 and t_ >= 2 else ()),
                     inc=f"d_x{t_ % 2}", inc_n=16)
        mk_xld[t_] = dm

    def emit_xbf(t_):
        w0 = ((("s_g", mk_gmem), ("d_vec", mk_vec)) if t_ < 2 else ())
        mk_xbf[t_] = add("s", lambda e, t_=t_: e.activation(
            out=xbf[t_ % 2], in_=x32[t_ % 2], func=AF.Identity, bias=zcol),
            waits=w0 + ((f"d_x{t_ % 2}", mk_xld[t_]),
                        ("s_t", lambda t_=t_: mk_xhmm.get((t_ - 2, 15), 0))),
            inc="s_s")

    emit_xload(0)
    emit_xbf(0)
    mk_sin2 = {-1: 0}

    def emit_lateS(t_):
        la = laS[t_ % 2]
        mk_rho[t_] = add("s", lambda e, la=la: e.activation(
            out=rho, in_=la, func=AF.Exp, scale=2.0),
            waits=(("s_v", lambda t_=t_: mk_laS[t_]),
                   ("s_v", lambda t_=t_: mk_ns[t_ - 1]),
                   ("s_v", lambda t_=t_: mk_thr2[t_ - 1])), inc="s_s")
        add("s", lambda e: e.activation(
            out=nsb, in_=rho, func=AF.Ln),
            waits=(("s_v", lambda t_=t_: mk_ns[t_]),), inc="s_s")
        mk_norm_d[t_] = add("s", lambda e: e.activation(
            out=normb, in_=nsb, func=AF.Exp, scale=0.5), inc="s_s")
        mk_s1_d[t_] = add("s", lambda e: e.activation(
            out=spS, in_=thr, func=AF.Sin, scale=float(2 * np.pi)),
            waits=(("s_v", lambda t_=t_: mk_thr1_d[t_]),
                   ("d_spl", lambda t_=t_: mk_spill[t_ - 1])), inc="s_s")
        mk_sin2[t_] = add("s", lambda e: e.activation(
            out=cpS, in_=thr2b, func=AF.Sin, scale=float(2 * np.pi)),
            waits=(("s_v", lambda t_=t_: mk_thr2[t_]),), inc="s_s")

    mk_norm_d = {}
    mk_s1_d = {}
    mk_thr1_d = {}

    for t_ in range(NTA):

        for m in range(16):
            waits = [("s_s", mk_xhev[(t_, m - 2)] if m >= 2
                      else mk_xhev.get((t_ - 1, 14 + m), 0))]
            if m == 0:
                waits.append(("s_s", mk_xbf[t_]))
                if t_ == 0:
                    waits.append(("s_v", mk_wready))
            mk = None
            for k in range(8):
                mk = add("t", lambda e, t_=t_, m=m, k=k: e.matmul(
                    out=ps[m % 2][:, :TA],
                    lhsT=w_in_bf[:, k * DH + m * 128:k * DH + (m + 1) * 128],
                    rhs=xbf[t_ % 2][:, k * TA:(k + 1) * TA],
                    start=(k == 0), stop=(k == 7)),
                    waits=tuple(waits) if k == 0 else (),
                    inc="s_t" if k == 7 else None)
                waits = []
            mk_xhmm[(t_, m)] = mk
            waits = [("s_t", mk_xhmm[(t_, m)])]
            if m == 0:
                waits += [("s_t", mk_gmm.get((t_ - 2, 15), 0)),
                          ("s_v", mk_rotV[t_ - 2]),
                          ("s_g", mk_rotG[t_ - 2])]
                if t_ == 0:
                    waits.append(("d_vec", mk_vec))
            mk_xhev[(t_, m)] = add("s", lambda e, t_=t_, m=m: e.activation(
                out=xh_sb[t_ % 2][:, m * TA:(m + 1) * TA],
                in_=ps[m % 2][:, :TA], func=AF.Identity,
                bias=vb_in[:, m:m + 1]),
                waits=tuple(waits), inc="s_s")

        for gi in range(2):
            for n in range(8):
                idx = gi * 8 + n
                waits = [("s_s", mk_xhev[(t_, 2 * n + 1)]),
                         ("s_s", mk_gev[(t_, idx - 2)] if idx >= 2
                          else mk_gev.get((t_ - 1, 14 + idx), 0))]
                wb = wgx_bf if gi == 0 else wga_bf
                mk = None
                for j in range(2):
                    kc = 2 * n + j
                    mk = add("t", lambda e, t_=t_, wb=wb, kc=kc, idx=idx:
                             e.matmul(
                                 out=ps[2 + idx % 2][:, :TA],
                                 lhsT=wb[:, kc * 128:(kc + 1) * 128],
                                 rhs=xh_sb[t_ % 2][:, kc * TA:(kc + 1) * TA],
                                 start=(kc % 2 == 0), stop=(kc % 2 == 1)),
                             waits=tuple(waits) if j == 0 else (),
                             inc="s_t" if j == 1 else None)
                    waits = []
                mk_gmm[(t_, idx)] = mk
                waits = [("s_t", mk_gmm[(t_, idx)])]
                if idx == 0:
                    waits.append(("s_v", mk_grd[t_ - 2]))
                gdst = gate_x[t_ % 2] if gi == 0 else gate_a[t_ % 2]
                gb = vb_gx if gi == 0 else vb_ga
                mk_gev[(t_, idx)] = add("s", lambda e, gdst=gdst, gb=gb, n=n,
                                        idx=idx: e.activation(
                    out=gdst[:, n * TA:(n + 1) * TA],
                    in_=ps[2 + idx % 2][:, :TA], func=AF.Sigmoid,
                    bias=gb[:, n:n + 1]),
                    waits=tuple(waits), inc="s_s")

        gates_done = mk_gev[(t_, 15)]
        if t_ > 0:
            emit_lateS(t_ - 1)

        ga = gate_a[t_ % 2]
        gx = gate_x[t_ % 2]
        la = laS[t_ % 2]
        mk = None
        for c in range(8):
            waits = []
            if c == 0:
                waits = [("s_s", gates_done), ("d_spl", mk_spill[t_ - 1])]
                if t_ == 0:
                    waits.append(("s_v", mk_m8sp))
            mk = add("v", lambda e, c=c, la=la, ga=ga: e.tensor_scalar(
                out=la[:, c * TA:(c + 1) * TA],
                in0=ga[:, c * TA:(c + 1) * TA],
                scalar1=vm8sp[:, c:c + 1], scalar2=None, op0=ALU.mult),
                waits=tuple(waits), inc="s_v")
        mk_laS[t_] = mk
        for c in range(8):
            add("v", lambda e, c=c, ga=ga: e.tensor_scalar(
                out=th[:, c * TA:(c + 1) * TA],
                in0=ga[:, c * TA:(c + 1) * TA],
                scalar1=vaimag[:, c:c + 1], scalar2=None, op0=ALU.mult),
                waits=((("d_vec", mk_vec),) if t_ == 0 and c == 0 else ()),
                inc="s_v")
        if t_ + 1 < NTA:
            emit_xload(t_ + 1)
            emit_xbf(t_ + 1)
        add("v", lambda e: e.tensor_scalar(
            out=nsb, in0=rho, scalar1=-1.0, scalar2=1.0,
            op0=ALU.mult, op1=ALU.add),
            waits=(("s_s", lambda t_=t_: mk_rho[t_]),), inc="s_v")
        mk_cl = add("v", lambda e: e.tensor_scalar(
            out=rho, in0=nsb, scalar1=1e-6, scalar2=None, op0=ALU.max),
            inc="s_v")
        mk_ns[t_] = mk_cl
        mk_g2 = add("v", lambda e, gx=gx: e.tensor_tensor(
            out=g2b, in0=gx, in1=normb, op=ALU.mult),
            waits=(("s_s", lambda t_=t_: mk_norm_d[t_]),), inc="s_v")
        mk_grd[t_] = mk_g2
        for c in range(8):
            waits = []
            if t_ == 0 and c == 0:
                waits = [("s_g", mk_gmem)]
            add("v", lambda e, c=c: e.tensor_tensor_scan(
                out=Th[:, c * TA:(c + 1) * TA], data0=onesA,
                data1=th[:, c * TA:(c + 1) * TA],
                initial=th_off[:, c:c + 1], op0=ALU.mult, op1=ALU.add),
                waits=tuple(waits), inc="s_v")
        add("v", lambda e: e.tensor_scalar(
            out=tsc, in0=Th, scalar1=INV2PI, scalar2=None, op0=ALU.mult),
            inc="s_v")
        add("v", lambda e: e.tensor_copy(out=kint, in_=tsc), inc="s_v")
        for c in range(8):
            add("v", lambda e, c=c: e.tensor_copy(
                out=th_off[:, c:c + 1],
                in_=Th[:, c * TA + TA - 1:c * TA + TA]), inc="s_v")
        add("v", lambda e: e.scalar_tensor_tensor(
            out=thr, in0=kint, scalar=-1.0, in1=tsc,
            op0=ALU.mult, op1=ALU.add),
            waits=(("s_s", lambda t_=t_: mk_sin2[t_ - 1]),), inc="s_v")
        add("v", lambda e: e.tensor_scalar(
            out=rho, in0=thr, scalar1=0.5, scalar2=None, op0=ALU.is_gt),
            inc="s_v")
        mk_thr1_d[t_] = add("v", lambda e: e.scalar_tensor_tensor(
            out=thr, in0=rho, scalar=-1.0, in1=thr,
            op0=ALU.mult, op1=ALU.add), inc="s_v")
        add("v", lambda e: e.tensor_scalar(
            out=tsc, in0=Th, scalar1=INV2PI, scalar2=0.25,
            op0=ALU.mult, op1=ALU.add), inc="s_v")
        add("v", lambda e: e.tensor_copy(out=kint, in_=tsc), inc="s_v")
        add("v", lambda e: e.scalar_tensor_tensor(
            out=thr2b, in0=kint, scalar=-1.0, in1=tsc,
            op0=ALU.mult, op1=ALU.add),
            waits=(("s_s", lambda t_=t_: mk_s1_d[t_]),), inc="s_v")
        add("v", lambda e: e.tensor_scalar(
            out=rho, in0=thr2b, scalar1=0.5, scalar2=None, op0=ALU.is_gt),
            inc="s_v")
        mk_thr2[t_] = add("v", lambda e: e.scalar_tensor_tensor(
            out=thr2b, in0=rho, scalar=-1.0, in1=thr2b,
            op0=ALU.mult, op1=ALU.add), inc="s_v")
        add("v", lambda e: e.tensor_tensor(
            out=ub, in0=g2b, in1=cpS, op=ALU.mult),
            waits=(("s_s", lambda t_=t_: mk_sin2[t_]),), inc="s_v")
        mk_uv = add("v", lambda e: e.tensor_tensor(
            out=vvb, in0=g2b, in1=spS, op=ALU.mult), inc="s_v")
        add("v", lambda e, t_=t_: e.tensor_tensor(
            out=m1b, in0=ub, in1=xh_sb[t_ % 2][:, 0:8 * TA], op=ALU.mult),
            inc="s_v")
        add("v", lambda e, t_=t_: e.tensor_tensor(
            out=m2b, in0=vvb, in1=xh_sb[t_ % 2][:, 8 * TA:16 * TA],
            op=ALU.mult), inc="s_v")
        mk_m3 = add("g", lambda e, t_=t_: e.tensor_tensor(
            out=m3b, in0=ub, in1=xh_sb[t_ % 2][:, 8 * TA:16 * TA],
            op=ALU.mult), waits=(("s_v", cnt["s_v"]),), inc="s_g")
        mk_m4 = add("v", lambda e, t_=t_: e.tensor_tensor(
            out=xtiS, in0=vvb, in1=xh_sb[t_ % 2][:, 0:8 * TA], op=ALU.mult),
            waits=(("d_spl", lambda t_=t_: mk_spill[t_ - 1]),), inc="s_v")
        mk_xtr = add("v", lambda e: e.tensor_tensor(
            out=xtrS, in0=m1b, in1=m2b, op=ALU.add), inc="s_v")
        mk_rotG[t_] = mk_m3
        mk_xti = add("v", lambda e: e.scalar_tensor_tensor(
            out=xtiS, in0=xtiS, scalar=-1.0, in1=m3b,
            op0=ALU.mult, op1=ALU.add),
            waits=(("s_g", mk_m3),), inc="s_v")
        mk_rotV[t_] = mk_xti

        mk = None
        for i, (stile, dram) in enumerate(((xtrS, sp_xtr), (xtiS, sp_xti),
                                           (cpS, sp_cp), (spS, sp_sp),
                                           (la, sp_la))):
            sv = stile.rearrange("p (c a) -> p c a", a=TA)
            dv = dram.ap()[:, :, t_ * TA:(t_ + 1) * TA].rearrange(
                "c p a -> p c a")
            for hf in range(2):
                mk = add("sp", lambda e, s=sv[:, hf * 4:(hf + 1) * 4, :],
                         d=dv[:, hf * 4:(hf + 1) * 4, :]: e.dma_start(
                    out=d, in_=s),
                    waits=(("s_v", mk_xti),) if (i == 0 and hf == 0) else (),
                    inc="d_spl", inc_n=16)
        mk_spill[t_] = mk

    emit_lateS(NTA - 1)
    A_done_v = cnt["s_v"]
    A_done_s = cnt["s_s"]
    A_done_t = cnt["s_t"]
    A_done_g = cnt["s_g"]
    A_done_dma = [(n, cnt[n]) for n in ("d_vec", "d_w0", "d_w1", "d_x0",
                                        "d_x1", "d_spl")]

    RB = Region(arena, PERSIST_END, ARENA)
    w_out_bf = RB.alloc(16 * D_, BF16)
    B2_START = RB.off
    wst2 = [RB.alloc(D_, F32) for _ in range(2)]
    bxtr = [RB.alloc(T, F16) for _ in range(3)]
    bxti = [RB.alloc(T, F16) for _ in range(3)]
    bcp = [RB.alloc(T, F16) for _ in range(3)]
    bsp = [RB.alloc(T, F16) for _ in range(3)]
    bla = [RB.alloc(T, F16) for _ in range(3)]
    brho = [RB.alloc(T, F32) for _ in range(2)]
    bgr = [RB.alloc(T, F16) for _ in range(2)]
    bgi = [RB.alloc(T, F16) for _ in range(2)]
    g0sc = bgr
    g0sd = bgi
    bh1 = [RB.alloc(T, BF16) for _ in range(2)]
    bh2 = [RB.alloc(T, BF16) for _ in range(2)]
    bh3 = [RB.alloc(T, BF16) for _ in range(2)]
    bh4 = [RB.alloc(T, BF16) for _ in range(2)]
    bhr = [RB.alloc(T, BF16) for _ in range(2)]
    bhi = [RB.alloc(T, BF16) for _ in range(2)]
    wcvI = wcvB = None

    barrier_B = tuple([("s_v", A_done_v), ("s_s", A_done_s),
                       ("s_t", A_done_t), ("s_g", A_done_g)] + A_done_dma)

    mk_wocv = {}
    wo_state = {"i": 0}

    def emit_wo(n):
        for _ in range(n):
            kc = wo_state["i"]
            if kc >= 16:
                return
            wo_state["i"] += 1
            waits = list(barrier_B) if kc == 0 else []
            if kc >= 2:
                waits.append(("s_s", mk_wocv[kc - 2]))
            mk = add("sp", lambda e, kc=kc: e.dma_start(
                out=wst2[kc % 2], in_=w_out_d[kc * 128:(kc + 1) * 128, :]),
                waits=tuple(waits), inc=f"d_wo{kc % 2}", inc_n=16)
            mk_wocv[kc] = add("s", lambda e, kc=kc: e.activation(
                out=w_out_bf[:, kc * D_:(kc + 1) * D_], in_=wst2[kc % 2],
                func=AF.Identity, bias=zcol),
                waits=((f"d_wo{kc % 2}", mk),), inc="s_s")

    wcv_jobs = []
    for dram, dstd, n1, wid in ((wd_d, wdb_d, 24, D_), (wg_d, wgb_d, 8, FH),
                                (wu_d, wub_d, 8, FH)):
        for k in range(n1):
            for j in range(0, wid, 1024):
                wcv_jobs.append((dram, dstd, k, j))
    mk_wcv_cv = {}
    mk_wcv_out = {-4: 0, -3: 0, -2: 0, -1: 0}
    wcv_state = {"i": 0}
    wcv_marks = {}

    def emit_wcv(n):
        for _ in range(n):
            widx = wcv_state["i"]
            if widx >= len(wcv_jobs):
                return
            wcv_state["i"] += 1
            dram, dstd, k, j = wcv_jobs[widx]
            sl = widx % 4
            sm = widx % 2
            waits = [("s_g", mk_wcv_cv[widx - 4])] if widx >= 4 else []
            if widx == 0:
                waits += list(barrier_B)
            mki = add("sp", lambda e, dram=dram, k=k, j=j, sl=sl:
                      e.dma_start(out=wcvI[sl],
                                  in_=dram[k * 128:(k + 1) * 128,
                                           j:j + 1024]),
                      waits=tuple(waits), inc=f"d_ci{sm}", inc_n=16)
            mk_wcv_cv[widx] = add("g", lambda e, sl=sl: e.tensor_copy(
                out=wcvB[sl], in_=wcvI[sl]),
                waits=((f"d_ci{sm}", mki),
                       (f"d_co{sm}", mk_wcv_out[widx - 4])), inc="s_g")
            mk_wcv_out[widx] = add("g", lambda e, dstd=dstd, k=k, j=j,
                                   sl=sl: e.dma_start(
                out=dstd.ap()[k][:, j:j + 1024], in_=wcvB[sl]),
                inc=f"d_co{sm}", inc_n=16)
            if widx == 23:
                wcv_marks["wd"] = (cnt["d_co0"], cnt["d_co1"])
            elif widx == 47:
                wcv_marks["wg"] = (cnt["d_co0"], cnt["d_co1"])
            elif widx == 71:
                wcv_marks["wu"] = (cnt["d_co0"], cnt["d_co1"])

    def gather(plane, c, dstbuf, queue, sem, extra_waits=()):
        return add(queue, lambda e, s=plane.ap()[c], d=dstbuf: e.dma_start(
            out=d, in_=s), waits=tuple(extra_waits), inc=sem, inc_n=16)

    mk_p1ld = {}
    mk_p1s = {}
    mk_p1 = {-3: 0, -2: 0, -1: 0}
    for c in range(8):
        w = [("s_v", mk_p1[c - 3])]
        if c == 0:
            w += list(barrier_B)
        sem = f"d_p1{c % 2}"
        gather(sp_xtr, c, bxtr[c % 3], "s", sem, extra_waits=w)
        gather(sp_xti, c, bxti[c % 3], "s", sem)
        mk_p1ld[c] = gather(sp_la, c, bla[c % 3], "s", sem)
        mk_p1s[c] = add("s", lambda e, c=c: e.activation(
            out=brho[c % 2], in_=bla[c % 3], func=AF.Exp),
            waits=((f"d_p1{c % 2}", mk_p1ld[c]), ("s_v", mk_p1[c - 2])),
            inc="s_s")
        add("v", lambda e, c=c: e.tensor_tensor_scan(
            out=g0sc[c % 2], data0=brho[c % 2], data1=bxtr[c % 3],
            initial=zcol[:, 0:1], op0=ALU.mult, op1=ALU.add),
            waits=(("s_s", mk_p1s[c]),), inc="s_v")
        add("v", lambda e, c=c: e.tensor_tensor_scan(
            out=g0sd[c % 2], data0=brho[c % 2], data1=bxti[c % 3],
            initial=zcol[:, 0:1], op0=ALU.mult, op1=ALU.add), inc="s_v")
        if c > 0:
            add("v", lambda e, c=c: e.tensor_copy(
                out=g0rL[:, c - 1:c], in_=g0sc[(c - 1) % 2][:, T - 1:T]),
                inc="s_v")
            mk_p1[c - 1] = add("v", lambda e, c=c: e.tensor_copy(
                out=g0iL[:, c - 1:c], in_=g0sd[(c - 1) % 2][:, T - 1:T]),
                inc="s_v")
        emit_wcv(3)
        emit_wo(2)
    emit_wo(16)
    mk_wo_done = mk_wocv[15]
    for _ in range(4):
        add("v", lambda e: e.tensor_copy(out=dum_b, in_=dum_a))
    add("v", lambda e: e.tensor_copy(
        out=g0rL[:, 7:8], in_=g0sc[1][:, T - 1:T]), inc="s_v")
    mk_p1[7] = add("v", lambda e: e.tensor_copy(
        out=g0iL[:, 7:8], in_=g0sd[1][:, T - 1:T]), inc="s_v")
    for _ in range(4):
        add("v", lambda e: e.tensor_copy(out=dum_b, in_=dum_a))

    cl_src = sp_cp.ap()[:, :, T - 1:T].rearrange("c p o -> p c o")
    sl_src = sp_sp.ap()[:, :, T - 1:T].rearrange("c p o -> p c o")
    mk_cpl = add("g", lambda e: e.dma_start(
        out=cpL.rearrange("p (c o) -> p c o", o=1), in_=cl_src,
        allow_slow_non_contiguous=True),
        waits=(("d_spl", mk_spill[NTA - 1]),), inc="d_cc", inc_n=16)
    mk_spl = add("g", lambda e: e.dma_start(
        out=spL.rearrange("p (c o) -> p c o", o=1), in_=sl_src,
        allow_slow_non_contiguous=True),
        inc="d_cc", inc_n=16)

    add("v", lambda e: e.tensor_tensor(out=car_a, in0=g0rL, in1=cpL,
                                       op=ALU.mult),
        waits=(("s_v", mk_p1[7]), ("d_cc", mk_spl)), inc="s_v")
    add("v", lambda e: e.tensor_tensor(out=car_b, in0=g0iL, in1=spL,
                                       op=ALU.mult), inc="s_v")
    for _ in range(4):
        add("v", lambda e: e.tensor_copy(out=dum_b, in_=dum_a))
    add("v", lambda e: e.tensor_tensor(out=carry_st[:, 0:8], in0=car_a,
                                       in1=car_b, op=ALU.subtract),
        inc="s_v")
    add("v", lambda e: e.tensor_tensor(out=car_a, in0=g0iL, in1=cpL,
                                       op=ALU.mult), inc="s_v")
    add("v", lambda e: e.tensor_tensor(out=car_b, in0=g0rL, in1=spL,
                                       op=ALU.mult), inc="s_v")
    for _ in range(4):
        add("v", lambda e: e.tensor_copy(out=dum_b, in_=dum_a))
    add("v", lambda e: e.tensor_tensor(
        out=carry_st[:, 8:16], in0=car_a, in1=car_b, op=ALU.add), inc="s_v")
    for _ in range(4):
        add("v", lambda e: e.tensor_copy(out=dum_b, in_=dum_a))
    mk_carry = add("v", lambda e: e.tensor_copy(out=dum_b, in_=dum_a),
                   inc="s_v")

    mk_ccw = add("g", lambda e: e.dma_start(out=cc_in.ap(), in_=carry_st),
                 waits=(("s_v", mk_carry),), inc="d_cc", inc_n=16)
    add("g", lambda e: e.collective_compute(
        "AllGather", ALU.bypass,
        replica_groups=[[0, 1], [2, 3], [4, 5], [6, 7]],
        ins=[cc_in.ap().opt()], outs=[cc_out.ap().opt()]),
        waits=(("d_cc", mk_ccw),), inc="cc")
    mk_ccr = add("g", lambda e: e.dma_start(out=carry_ga, in_=cc_out.ap()[0]),
                 waits=(("cc", 1),), inc="d_cc", inc_n=16)
    add("v", lambda e: e.tensor_scalar(
        out=carry_eff, in0=carry_ga, scalar1=vmask[:, 0:1], scalar2=None,
        op0=ALU.mult),
        waits=(("d_cc", mk_ccr), ("d_vec", mk_vec)), inc="s_v")
    for _ in range(10):
        add("v", lambda e: e.tensor_copy(out=dum_b, in_=dum_a))

    mk_p2ld = {}
    mk_p2s = {}
    mk_p2scan = {-2: 0, -1: 0}
    mk_p2V = {-2: 0, -1: 0}
    mk_p2G = {-2: 0, -1: 0}
    mk_u2 = {}
    mk_hsp = {-2: 0, -1: 0}

    def emit_comb(c):
        mk_hi = None
        mk_hr = add("v", lambda e, c=c: e.tensor_tensor(
            out=bhr[c % 2], in0=bh1[c % 2], in1=bh2[c % 2],
            op=ALU.subtract),
            waits=(("s_g", mk_u2[c]), (f"d_h{c % 2}", mk_hsp[c - 2])),
            inc="s_v")
        mk_hi = add("v", lambda e, c=c: e.tensor_tensor(
            out=bhi[c % 2], in0=bh3[c % 2], in1=bh4[c % 2], op=ALU.add),
            waits=(("s_g", mk_p2G[c]),), inc="s_v")
        mk_p2V[c] = mk_hi
        add("s", lambda e, c=c: e.dma_start(out=sp_h.ap()[c],
                                            in_=bhr[c % 2]),
            waits=(("s_v", mk_hi),), inc=f"d_h{c % 2}", inc_n=16)
        mk_hsp[c] = add("s", lambda e, c=c: e.dma_start(
            out=sp_h.ap()[8 + c], in_=bhi[c % 2]), inc=f"d_h{c % 2}",
            inc_n=16)

    mk_p2V[-3] = mk_p2G[-3] = 0
    for c in range(8):
        w = [("s_v", mk_p2V[c - 3]), ("s_g", mk_p2G[c - 3])]
        if c == 0:
            w.append(("s_v", mk_p1[7]))
        sem = f"d_p2{c % 2}"
        gather(sp_xtr, c, bxtr[c % 3], "sp", sem, extra_waits=w)
        gather(sp_xti, c, bxti[c % 3], "sp", sem)
        gather(sp_cp, c, bcp[c % 3], "sp", sem)
        gather(sp_sp, c, bsp[c % 3], "sp", sem)
        mk_p2ld[c] = gather(sp_la, c, bla[c % 3], "sp", sem)
        mk_p2s[c] = add("s", lambda e, c=c: e.activation(
            out=brho[c % 2], in_=bla[c % 3], func=AF.Exp),
            waits=((f"d_p2{c % 2}", mk_p2ld[c]), ("s_v", mk_p2scan[c - 2]),
                   ("s_s", mk_wo_done if c == 0 else 0)), inc="s_s")
        add("v", lambda e, c=c: e.tensor_tensor_scan(
            out=bgr[c % 2], data0=brho[c % 2], data1=bxtr[c % 3],
            initial=carry_eff[:, c:c + 1], op0=ALU.mult, op1=ALU.add),
            waits=(("s_s", mk_p2s[c]), ("s_g", mk_p2G[c - 2])), inc="s_v")
        mk_p2scan[c] = add("v", lambda e, c=c: e.tensor_tensor_scan(
            out=bgi[c % 2], data0=brho[c % 2], data1=bxti[c % 3],
            initial=carry_eff[:, 8 + c:9 + c], op0=ALU.mult, op1=ALU.add),
            inc="s_v")
        add("v", lambda e, c=c: e.tensor_tensor(
            out=bh1[c % 2], in0=bgr[c % 2], in1=bcp[c % 3], op=ALU.mult),
            inc="s_v")
        mk_u2[c] = add("g", lambda e, c=c: e.tensor_tensor(
            out=bh2[c % 2], in0=bgi[c % 2], in1=bsp[c % 3], op=ALU.mult),
            waits=(("s_v", mk_p2scan[c]),), inc="s_g")
        add("g", lambda e, c=c: e.tensor_tensor(
            out=bh3[c % 2], in0=bgi[c % 2], in1=bcp[c % 3], op=ALU.mult),
            inc="s_g")
        mk_p2G[c] = add("g", lambda e, c=c: e.tensor_tensor(
            out=bh4[c % 2], in0=bgr[c % 2], in1=bsp[c % 3], op=ALU.mult),
            inc="s_g")
        if c > 0:
            emit_comb(c - 1)
    emit_comb(7)

    B1_done_v = cnt["s_v"]
    B1_done_s = cnt["s_s"]
    B1_done_g = cnt["s_g"]
    B1_done_dma = [(n, cnt[n]) for n in ("d_wo0", "d_wo1", "d_p10", "d_p11",
                                         "d_p20", "d_p21", "d_cc", "d_h0",
                                         "d_h1")]

    R2 = Region(arena, B2_START, ARENA)
    bhin = [R2.alloc(16 * TB, BF16) for _ in range(2)]
    bx2 = [R2.alloc(8 * TB, F32) for _ in range(2)]
    br1 = R2.alloc(8 * TB, F32)
    br1b = R2.alloc(8 * TB, BF16)
    bsq = R2.alloc(8 * TB, BF16)
    bmean = R2.alloc(TB, F32)
    bmm = R2.alloc(TB, F32)
    bvar = R2.alloc(TB, F32)
    bstd = R2.alloc(TB, F32)
    brstd = R2.alloc(TB, F32)
    btd = R2.alloc(8 * TB, F32)
    bx1 = [R2.alloc(8 * TB, BF16) for _ in range(2)]
    RW = Region(arena, max(R2.off, RB.off), ARENA)
    wcvI = [RW.alloc(1024, I32) for _ in range(4)]
    wcvB = [RW.alloc(1024, BF16) for _ in range(4)]

    barrier_B2 = tuple([("s_v", B1_done_v), ("s_s", B1_done_s),
                        ("s_g", B1_done_g)] + B1_done_dma)
    mk_hld = {}
    mk_xrl = {}
    mk_ymm = {}
    mk_yev = {}
    mk_x1 = {}
    mk_x1sp = {-2: 0, -1: 0}
    mk_lnev = {-1: 0}
    first_b2 = True
    for t_ in range(NTB):
        waits = list(barrier_B2) if first_b2 else []
        if t_ >= 2:
            waits.append(("s_t", mk_ymm[(t_ - 2, 7)]))
        mk = None
        for kc in range(16):
            mk = add("sp", lambda e, t_=t_, kc=kc: e.dma_start(
                out=bhin[t_ % 2][:, kc * TB:(kc + 1) * TB],
                in_=sp_h.ap()[kc][:, t_ * TB:(t_ + 1) * TB]),
                waits=tuple(waits) if kc == 0 else (),
                inc=f"d_b2{t_ % 2}", inc_n=16)
            waits = []
            first_b2 = False
        mk_hld[t_] = mk
        mk = None
        for half in range(2):
            src = x_d[half * 512:(half + 1) * 512,
                      t_ * TB:(t_ + 1) * TB].rearrange("(c p) t -> p c t",
                                                       p=128)
            dst = bx2[t_ % 2][:, half * 4 * TB:(half + 1) * 4 * TB].rearrange(
                "p (c t) -> p c t", t=TB)
            w2 = (("s_v", mk_yev.get((t_ - 2, 7), 0)),) if half == 0 else ()
            mk = add("sp", lambda e, s=src, d=dst: e.dma_start(out=d, in_=s),
                     waits=w2, inc=f"d_xr{t_ % 2}", inc_n=16)
        mk_xrl[t_] = mk

        for m in range(8):
            waits = [("s_v", mk_yev[(t_, m - 2)] if m >= 2
                      else mk_yev.get((t_ - 1, 6 + m), 0))]
            if m == 0:
                waits.append((f"d_b2{t_ % 2}", mk_hld[t_]))
                if t_ == 0:
                    waits.append(("s_s", mk_wo_done))
            mk = None
            for kc in range(16):
                mk = add("t", lambda e, t_=t_, m=m, kc=kc: e.matmul(
                    out=ps[m % 2][:, :TB],
                    lhsT=w_out_bf[:, kc * D_ + m * 128:
                                  kc * D_ + (m + 1) * 128],
                    rhs=bhin[t_ % 2][:, kc * TB:(kc + 1) * TB],
                    start=(kc == 0), stop=(kc == 15)),
                    waits=tuple(waits) if kc == 0 else (),
                    inc="s_t" if kc == 15 else None)
                waits = []
            mk_ymm[(t_, m)] = mk
            waits = [("s_t", mk_ymm[(t_, m)])]
            if m == 0:
                waits.append((f"d_xr{t_ % 2}", mk_xrl[t_]))
            mk_yev[(t_, m)] = add("v", lambda e, t_=t_, m=m:
                                  e.scalar_tensor_tensor(
                out=br1[:, m * TB:(m + 1) * TB], in0=ps[m % 2][:, :TB],
                scalar=vb_out[:, m:m + 1],
                in1=bx2[t_ % 2][:, m * TB:(m + 1) * TB],
                op0=ALU.add, op1=ALU.add),
                waits=tuple(waits), inc="s_v")
        add("v", lambda e: e.tensor_copy(out=br1b, in_=br1), inc="s_v")
        mk_sq = add("v", lambda e: e.tensor_tensor(
            out=bsq, in0=br1b, in1=br1b, op=ALU.mult), inc="s_v")
        waits = [("s_v", mk_sq), ("s_s", mk_lnev[t_ - 1])]
        for kc in range(8):
            add("t", lambda e, kc=kc: e.matmul(
                out=ps[2][:, :TB], lhsT=ones_bf,
                rhs=br1b[:, kc * TB:(kc + 1) * TB],
                start=(kc == 0), stop=(kc == 7)),
                waits=tuple(waits) if kc == 0 else ())
            waits = []
        mk = None
        for kc in range(8):
            mk = add("t", lambda e, kc=kc: e.matmul(
                out=ps[3][:, :TB], lhsT=ones_bf,
                rhs=bsq[:, kc * TB:(kc + 1) * TB],
                start=(kc == 0), stop=(kc == 7)),
                inc="s_t" if kc == 7 else None)
        mk_lnmm = mk
        mk_mean = add("s", lambda e: e.activation(
            out=bmean, in_=ps[2][:, :TB], func=AF.Identity,
            scale=float(1.0 / 1024.0), bias=zcol),
            waits=(("s_t", mk_lnmm),), inc="s_s")
        add("v", lambda e: e.tensor_tensor(
            out=bmm, in0=bmean, in1=bmean, op=ALU.mult),
            waits=(("s_s", mk_mean),), inc="s_v")
        mk_var = add("v", lambda e: e.scalar_tensor_tensor(
            out=bvar, in0=ps[3][:, :TB], scalar=float(1.0 / 1024.0),
            in1=bmm, op0=ALU.mult, op1=ALU.subtract), inc="s_v")
        mk_lnev[t_] = add("s", lambda e: e.activation(
            out=bstd, in_=bvar, func=AF.Sqrt, bias=epsln),
            waits=(("s_v", mk_var),), inc="s_s")
        add("v", lambda e: e.reciprocal(out=brstd, in_=bstd),
            waits=(("s_s", mk_lnev[t_]),), inc="s_v")
        for m in range(8):
            add("v", lambda e, m=m: e.scalar_tensor_tensor(
                out=btd[:, m * TB:(m + 1) * TB], in0=bmean, scalar=-1.0,
                in1=br1[:, m * TB:(m + 1) * TB], op0=ALU.mult, op1=ALU.add),
                inc="s_v")
        mk = None
        for m in range(8):
            w3 = ((f"d_xs{t_ % 2}", mk_x1sp[t_ - 2]),) if m == 0 else ()
            mk = add("v", lambda e, t_=t_, m=m: e.tensor_tensor(
                out=bx1[t_ % 2][:, m * TB:(m + 1) * TB],
                in0=btd[:, m * TB:(m + 1) * TB], in1=brstd, op=ALU.mult),
                waits=w3, inc="s_v")
        mk_x1[t_] = mk
        mk_x1sp[t_] = add("s", lambda e, t_=t_: e.dma_start(
            out=sp_x1.ap()[t_], in_=bx1[t_ % 2]),
            waits=(("s_v", mk_x1[t_]),), inc=f"d_xs{t_ % 2}", inc_n=16)
        emit_wcv(12)

    emit_wcv(len(wcv_jobs))
    B2_done_v = cnt["s_v"]
    B2_done_s = cnt["s_s"]
    B2_done_t = cnt["s_t"]
    B2_done_dma = [(n, cnt[n]) for n in ("d_b20", "d_b21", "d_xr0", "d_xr1",
                                         "d_xs0", "d_xs1")]

    RC = Region(arena, PERSIST_END, ARENA)
    wgb = RC.alloc(8 * FH, BF16)
    wub = RC.alloc(8 * FH, BF16)
    wdr = [RC.alloc(4 * D_, BF16) for _ in range(2)]
    bx1in = [RC.alloc(8 * TC, BF16) for _ in range(2)]
    BHF_OFF = RC.off
    bhf = RC.alloc(24 * TC, BF16)
    bsil = [RC.alloc(TC, F32) for _ in range(2)]
    bh1f = RC.alloc(TC, F32)
    br2 = RC.alloc(8 * TC, F32)
    br2b = RC.alloc(8 * TC, BF16)
    bsq2 = RC.alloc(8 * TC, BF16)
    cmean = RC.alloc(TC, F32)
    cmm = RC.alloc(TC, F32)
    cvar = RC.alloc(TC, F32)
    crstd = RC.alloc(TC, F32)
    cstd = cmm
    btd2 = RC.alloc(8 * TC, F32, at=BHF_OFF)

    barrier_C = tuple([("s_v", B2_done_v), ("s_s", B2_done_s),
                       ("s_t", B2_done_t)] + B2_done_dma)
    mk_wgld = mk_wuld = None
    for k in range(8):
        w = (list(barrier_C) + [("d_co0", wcv_marks["wg"][0]),
                                ("d_co1", wcv_marks["wg"][1])]
             if k == 0 else [])
        mk_wgld = add("s", lambda e, k=k: e.dma_start(
            out=wgb[:, k * FH:(k + 1) * FH], in_=wgb_d.ap()[k]),
            waits=tuple(w), inc="d_cw0", inc_n=16)
    for k in range(8):
        w = ([("d_co0", wcv_marks["wu"][0]), ("d_co1", wcv_marks["wu"][1])]
             if k == 0 else [])
        mk_wuld = add("s", lambda e, k=k: e.dma_start(
            out=wub[:, k * FH:(k + 1) * FH], in_=wub_d.ap()[k]),
            waits=tuple(w), inc="d_cw1", inc_n=16)

    mk_x1ld = {}
    mk_gu = {}
    mk_sil = {}
    mk_h = {}
    mk_dev = {}
    mk_dng = {}
    mk_wdld = {}
    mk_out = {-2: 0, -1: 0}
    mk_mean2 = {-1: 0}
    mk_var2 = {-1: 0}
    mk_lnev2 = {-1: 0}
    first_csp = True
    for t_ in range(NTC):
        w2 = [(f"d_xs{t_ % 2}", cnt[f"d_xs{t_ % 2}"]),
              ("s_v", mk_dev.get((t_ - 2, 7), 0))]
        if first_csp:
            w2 += list(barrier_C)
            first_csp = False
        mk_x1ld[t_] = add("sp", lambda e, t_=t_: e.dma_start(
            out=bx1in[t_ % 2], in_=sp_x1.ap()[t_]),
            waits=tuple(w2), inc=f"d_c1{t_ % 2}", inc_n=16)

        for g in range(6):
            wdsrc = wdb_d.ap()[g * 4:(g + 1) * 4].rearrange("g p d -> p g d")
            w3 = [("s_t", lambda t_=t_, g=g: mk_dng.get(
                (t_, g - 2), mk_dng.get((t_ - 1, g + 4), 0)))]
            if t_ == 0 and g == 0:
                w3 += [("d_co0", wcv_marks["wd"][0]),
                       ("d_co1", wcv_marks["wd"][1])]
            mk_wdld[(t_, g)] = add("sp", lambda e, s=wdsrc, g=g: e.dma_start(
                out=wdr[g % 2].rearrange("p (g d) -> p g d", d=D_), in_=s),
                waits=tuple(w3), inc=f"d_wd{g % 2}", inc_n=16)

        for hc in range(24):
            waits = [("s_s", mk_sil[(t_, hc - 2)] if hc >= 2
                      else mk_sil.get((t_ - 1, 22 + hc), 0)),
                     ("s_v", mk_h[(t_, hc - 2)] if hc >= 2
                      else mk_h.get((t_ - 1, 22 + hc), 0))]
            if hc == 0:
                waits.append((f"d_c1{t_ % 2}", mk_x1ld[t_]))
                waits.append(("s_v", mk_dev.get((t_ - 1, 3), 0)))
                waits.append(("s_s", mk_mean2[t_ - 1]))
                waits.append(("s_v", mk_var2[t_ - 1]))
                if t_ == 0:
                    waits.append(("d_cw0", mk_wgld))
            for kc in range(8):
                add("t", lambda e, t_=t_, hc=hc, kc=kc: e.matmul(
                    out=ps[hc % 2][:, :TC],
                    lhsT=wgb[:, kc * FH + hc * 128:kc * FH + (hc + 1) * 128],
                    rhs=bx1in[t_ % 2][:, kc * TC:(kc + 1) * TC],
                    start=(kc == 0), stop=(kc == 7)),
                    waits=tuple(waits) if kc == 0 else ())
                waits = []
            if t_ == 0 and hc == 0:
                waits = [("d_cw1", mk_wuld)]
            mk = None
            for kc in range(8):
                mk = add("t", lambda e, t_=t_, hc=hc, kc=kc: e.matmul(
                    out=ps[2 + hc % 2][:, :TC],
                    lhsT=wub[:, kc * FH + hc * 128:kc * FH + (hc + 1) * 128],
                    rhs=bx1in[t_ % 2][:, kc * TC:(kc + 1) * TC],
                    start=(kc == 0), stop=(kc == 7)),
                    waits=tuple(waits) if kc == 0 else (),
                    inc="s_t" if kc == 7 else None)
                waits = []
            mk_gu[(t_, hc)] = mk
            mk_sil[(t_, hc)] = add("s", lambda e, hc=hc: e.activation(
                out=bsil[hc % 2], in_=ps[hc % 2][:, :TC], func=AF.Sigmoid,
                scale=SC127),
                waits=(("s_t", mk_gu[(t_, hc)]),
                       ("s_v", mk_h[(t_, hc - 2)] if hc >= 2
                        else mk_h.get((t_ - 1, 22 + hc), 0))), inc="s_s")
            add("v", lambda e, hc=hc: e.scalar_tensor_tensor(
                out=bh1f, in0=ps[hc % 2][:, :TC], scalar=SC127,
                in1=bsil[hc % 2], op0=ALU.mult, op1=ALU.mult),
                waits=(("s_s", mk_sil[(t_, hc)]),), inc="s_v")
            mk_h[(t_, hc)] = add("v", lambda e, hc=hc: e.scalar_tensor_tensor(
                out=bhf[:, hc * TC:(hc + 1) * TC], in0=ps[2 + hc % 2][:, :TC],
                scalar=SC127, in1=bh1f, op0=ALU.mult, op1=ALU.mult),
                inc="s_v")

        mk_acc = {}
        for g in range(6):
            gw = [(f"d_wd{g % 2}", mk_wdld[(t_, g)])]
            if g == 0:
                gw.append(("s_v", mk_h[(t_, 23)]))
                for m in range(8):
                    gw.append(("s_v", mk_dev.get((t_ - 1, m), 0)))
                gw.append(("s_s", mk_mean2[t_ - 1]))
                gw.append(("s_v", mk_var2[t_ - 1]))
            mk = None
            for m in range(8):
                waits = list(gw) if m == 0 else []
                if g > 0:
                    waits.append(mk_acc[(g - 1, m)])
                for kc in range(4):
                    mk = add("t", lambda e, t_=t_, g=g, m=m, kc=kc: e.matmul(
                        out=ps[m][:, :TC],
                        lhsT=wdr[g % 2][:, kc * D_ + m * 128:
                                        kc * D_ + (m + 1) * 128],
                        rhs=bhf[:, (g * 4 + kc) * TC:(g * 4 + kc + 1) * TC],
                        start=(kc == 0), stop=(kc == 3)),
                        waits=tuple(waits) if kc == 0 else (),
                        inc="s_t" if kc == 3 else None)
                    waits = []
                if g == 0:
                    c = add("s", lambda e, m=m: e.activation(
                        out=br2[:, m * TC:(m + 1) * TC], in_=ps[m][:, :TC],
                        func=AF.Identity, bias=zcol),
                        waits=(("s_t", mk),), inc="s_s")
                    mk_acc[(g, m)] = ("s_s", c)
                else:
                    c = add("v", lambda e, m=m: e.scalar_tensor_tensor(
                        out=br2[:, m * TC:(m + 1) * TC], in0=ps[m][:, :TC],
                        scalar=1.0, in1=br2[:, m * TC:(m + 1) * TC],
                        op0=ALU.mult, op1=ALU.add),
                        waits=(("s_t", mk), mk_acc[(g - 1, m)]), inc="s_v")
                    mk_acc[(g, m)] = ("s_v", c)
            mk_dng[(t_, g)] = mk
        for m in range(8):
            w4 = [mk_acc[(5, m)]]
            if m == 0:
                w4.append((f"d_o{(t_ - 1) % 2}", mk_out[t_ - 1]))
            mk_dev[(t_, m)] = add("v", lambda e, t_=t_, m=m:
                                  e.scalar_tensor_tensor(
                out=br2[:, m * TC:(m + 1) * TC], in0=br2[:, m * TC:(m + 1) * TC],
                scalar=SC127, in1=bx1in[t_ % 2][:, m * TC:(m + 1) * TC],
                op0=ALU.mult, op1=ALU.add),
                waits=tuple(w4), inc="s_v")

        if debug:
            add("g", lambda e, t_=t_: e.dma_start(
                out=nc.dram_tensor(f"dbg_r2_{t_}", [128, 8 * TC], F32,
                                   kind="ExternalOutput").ap(),
                in_=br2), waits=(("s_v", mk_dev[(t_, 7)]),),
                inc="d_dbg", inc_n=16)
        add("s", lambda e: e.activation(out=br2b, in_=br2,
                                        func=AF.Identity, bias=zcol),
            waits=(("s_v", mk_dev[(t_, 7)]),), inc="s_s")
        mk_sq2 = add("s", lambda e: e.activation(
            out=bsq2, in_=br2, func=AF.Square), inc="s_s")
        waits = [("s_s", mk_sq2), ("s_s", mk_lnev2[t_ - 1])]
        for kc in range(8):
            add("t", lambda e, kc=kc: e.matmul(
                out=ps[0][:, :TC], lhsT=ones_bf,
                rhs=br2b[:, kc * TC:(kc + 1) * TC],
                start=(kc == 0), stop=(kc == 7)),
                waits=tuple(waits) if kc == 0 else ())
            waits = []
        mk = None
        for kc in range(8):
            mk = add("t", lambda e, kc=kc: e.matmul(
                out=ps[1][:, :TC], lhsT=ones_bf,
                rhs=bsq2[:, kc * TC:(kc + 1) * TC],
                start=(kc == 0), stop=(kc == 7)),
                inc="s_t" if kc == 7 else None)
        mk_lnmm2 = mk
        mk_mean2[t_] = add("s", lambda e: e.activation(
            out=cmean, in_=ps[0][:, :TC], func=AF.Identity,
            scale=float(1.0 / 1024.0), bias=zcol),
            waits=(("s_t", mk_lnmm2),), inc="s_s")
        add("v", lambda e: e.tensor_tensor(
            out=cmm, in0=cmean, in1=cmean, op=ALU.mult),
            waits=(("s_s", mk_mean2[t_]),), inc="s_v")
        mk_var2[t_] = add("v", lambda e: e.scalar_tensor_tensor(
            out=cvar, in0=ps[1][:, :TC], scalar=float(1.0 / 1024.0),
            in1=cmm, op0=ALU.mult, op1=ALU.subtract), inc="s_v")
        mk_lnev2[t_] = add("s", lambda e: e.activation(
            out=cstd, in_=cvar, func=AF.Sqrt, bias=epsln),
            waits=(("s_v", mk_var2[t_]),), inc="s_s")
        add("v", lambda e: e.reciprocal(out=crstd, in_=cstd),
            waits=(("s_s", mk_lnev2[t_]),), inc="s_v")
        for m in range(8):
            add("v", lambda e, m=m: e.scalar_tensor_tensor(
                out=btd2[:, m * TC:(m + 1) * TC], in0=cmean, scalar=-1.0,
                in1=br2[:, m * TC:(m + 1) * TC], op0=ALU.mult, op1=ALU.add),
                inc="s_v")
        mk = None
        for m in range(8):
            w5 = (("d_dbg", cnt["d_dbg"]),) if (debug and m == 0) else ()
            mk = add("v", lambda e, m=m: e.tensor_tensor(
                out=br2[:, m * TC:(m + 1) * TC],
                in0=btd2[:, m * TC:(m + 1) * TC], in1=crstd, op=ALU.mult),
                waits=w5, inc="s_v")
        mk_oev = mk
        mk = None
        for half in range(2):
            srcb = br2[:, half * 4 * TC:(half + 1) * 4 * TC].rearrange(
                "p (c t) -> p c t", t=TC)
            dstb = out_d[half * 512:(half + 1) * 512,
                         t_ * TC:(t_ + 1) * TC].rearrange("(c p) t -> p c t",
                                                          p=128)
            mk = add("g", lambda e, s=srcb, d=dstb: e.dma_start(out=d, in_=s),
                     waits=(("s_v", mk_oev),) if half == 0 else (),
                     inc=f"d_o{t_ % 2}", inc_n=16)
        mk_out[t_] = mk

    if debug:
        dbg_g = nc.dram_tensor("dbg_g0", [128, 4 * T + 16], F32,
                               kind="ExternalOutput").ap()
        add("g", lambda e: e.dma_start(out=dbg_g[:, 0:T], in_=g0sc[1]),
            waits=(("d_o0", cnt["d_o0"]), ("d_o1", cnt["d_o1"])),
            inc="d_dbg", inc_n=16)
        add("g", lambda e: e.dma_start(out=dbg_g[:, T:2 * T], in_=g0sd[1]),
            inc="d_dbg", inc_n=16)
        add("g", lambda e: e.dma_start(out=dbg_g[:, 2 * T:2 * T + 8],
                                       in_=g0rL), inc="d_dbg", inc_n=16)
        add("g", lambda e: e.dma_start(out=dbg_g[:, 2 * T + 8:2 * T + 16],
                                       in_=g0iL), inc="d_dbg", inc_n=16)
        dbg_d = nc.dram_tensor("dbg_carry", [128, 48], F32,
                               kind="ExternalOutput").ap()
        add("g", lambda e: e.dma_start(out=dbg_d[:, 0:16], in_=carry_st),
            waits=(("d_o0", cnt["d_o0"]), ("d_o1", cnt["d_o1"])),
            inc="d_dbg", inc_n=16)
        add("g", lambda e: e.dma_start(out=dbg_d[:, 16:32], in_=carry_eff),
            inc="d_dbg", inc_n=16)
        add("g", lambda e: e.dma_start(
            out=dbg_d[:, 32:36], in_=cpL.bitcast(F32)[:, 0:4]),
            inc="d_dbg", inc_n=16)
        add("g", lambda e: e.dma_start(
            out=dbg_d[:, 36:40], in_=spL.bitcast(F32)[:, 0:4]),
            inc="d_dbg", inc_n=16)

    add("g", lambda e: e.memset(dum_b, 0.0),
        waits=(("d_o0", cnt["d_o0"]), ("d_o1", cnt["d_o1"]),
               ("d_dbg", cnt["d_dbg"])))

    with nc.Block() as block:
        def emit(eng_key):
            def go(engine):
                for fn, waits, inc, inc_n in ops[eng_key]:
                    for wname, wcnt in waits:
                        if callable(wcnt):
                            wcnt = wcnt()
                        if wcnt > 0:
                            engine.wait_ge(sems[wname], wcnt)
                    instr = fn(engine)
                    if inc is not None:
                        if instr is None:
                            engine.sem_inc(sems[inc], inc_n)
                        else:
                            instr.then_inc(sems[inc], inc_n)
            return go

        block.sync(emit("sp"))
        block.tensor(emit("t"))
        block.vector(emit("v"))
        block.scalar(emit("s"))
        block.gpsimd(emit("g"))

    return nc


_prog_cache = {}


def _get_prog(debug=False):
    if debug not in _prog_cache:
        _prog_cache[debug] = build_program(debug)
    return _prog_cache[debug]


def _vec128(a, n):
    return np.ascontiguousarray(np.asarray(a, np.float32).reshape(n, 128).T)


def kernel(L0_attn_x, L0_attn_w_in, L0_attn_b_in, L0_attn_w_gx, L0_attn_b_gx,
           L0_attn_w_ga, L0_attn_b_ga, L0_attn_a_real_param,
           L0_attn_a_imag_param, L0_attn_w_out, L0_attn_b_out,
           L0_ffn_w_gate, L0_ffn_w_up, L0_ffn_w_down, _debug=False):
    x = np.asarray(L0_attn_x, np.float32)
    shared = {
        "w_in": np.ascontiguousarray(np.asarray(L0_attn_w_in, np.float32)),
        "w_gx": np.ascontiguousarray(
            np.asarray(L0_attn_w_gx, np.float32).reshape(DH, 128)),
        "w_ga": np.ascontiguousarray(
            np.asarray(L0_attn_w_ga, np.float32).reshape(DH, 128)),
        "w_out": np.ascontiguousarray(np.asarray(L0_attn_w_out, np.float32)),
        "wg_t": np.ascontiguousarray(np.asarray(L0_ffn_w_gate, np.int32).T),
        "wu_t": np.ascontiguousarray(np.asarray(L0_ffn_w_up, np.int32).T),
        "wd_t": np.ascontiguousarray(np.asarray(L0_ffn_w_down, np.int32).T),
        "b_in": _vec128(L0_attn_b_in, 16),
        "b_gx": _vec128(L0_attn_b_gx, 8),
        "b_ga": _vec128(L0_attn_b_ga, 8),
        "b_out": _vec128(L0_attn_b_out, 8),
        "aimag": _vec128(L0_attn_a_imag_param, 8),
        "arp": _vec128(L0_attn_a_real_param, 8),
    }
    in_maps = []
    for k in range(8):
        b, half = k // 2, k % 2
        m = dict(shared)
        m["x"] = np.ascontiguousarray(x[b, half * T:(half + 1) * T, :].T)
        m["mask"] = np.full((128, 1), float(half), np.float32)
        in_maps.append(m)

    nc = _get_prog(_debug)
    res = run_bass_kernel_spmd(nc, in_maps, core_ids=list(range(8)))

    y = np.empty((B_, S_, D_), np.float32)
    for k in range(8):
        b, half = k // 2, k % 2
        y[b, half * T:(half + 1) * T, :] = res.results[k]["out"].T
    if _debug:
        return y, res
    return y
